# version 12
# baseline (speedup 1.0000x reference)
import numpy as np
import concourse.bass as bass
import concourse.mybir as mybir
from concourse.bass_utils import run_bass_kernel_spmd

F32 = mybir.dt.float32
BF16 = mybir.dt.bfloat16
I8 = mybir.dt.int8
I32 = mybir.dt.int32
AF = mybir.ActivationFunctionType
OP = mybir.AluOpType
AX = mybir.AxisListType.X

D = 1024
DC = 8
TC = 512
TT = 4
NCH = 8
TLOC = NCH * TC
N16 = 16
INNER = 4096
IC = 32
TSUB = 256
NSUB = TC // TSUB
SLAB = N16 * (TSUB + 1)
MAGIC = float(np.float32(12582912.0))
EPS = 1e-5
STAGES = 4
DBG_CHUNKS = NCH
DBG_CUT = 99
DBG_DUMP = False


class VP:
    cols = {}
    nv = 0

    @classmethod
    def add(cls, name, w):
        cls.cols[name] = (cls.nv, w)
        cls.nv += w


VP.add("gamma_a", DC); VP.add("beta_a", DC)
VP.add("conv_w", 3 * DC)
VP.add("conv_b", DC)
VP.add("b_dt", DC)
VP.add("A_a", DC)
VP.add("Dsk_a", DC)
VP.add("gamma_b", DC); VP.add("beta_b", DC)
VP.add("b_delta", DC)
VP.add("Dsk_b", DC)
VP.add("A_logb", DC * N16)
VP.add("bcb", 1)
VP.add("hm", 2)
VP.add("msk", 1)
VP.add("onescol", 1)
NV = VP.nv


def build_nc(stages=STAGES, dbg_chunks=None, dbg_cut=None):
    global DBG_CHUNKS, DBG_CUT
    if dbg_chunks is not None:
        DBG_CHUNKS = dbg_chunks
    if dbg_cut is not None:
        DBG_CUT = dbg_cut
    nc = bass.Bass("TRN2", target_bir_lowering=False)

    x_sh = nc.dram_tensor("x_sh", [TLOC + 2, D], F32, kind="ExternalInput").ap()
    out_sh = nc.dram_tensor("out_sh", [TLOC, D], F32, kind="ExternalOutput").ap()
    dbg_xi = nc.dram_tensor("dbg_xi", [128, 4096], F32, kind="ExternalOutput").ap()
    dbg_gu = nc.dram_tensor("dbg_gu", [128, 16384], F32, kind="ExternalOutput").ap()
    dbg_r2 = nc.dram_tensor("dbg_r2", [128, 4096], F32, kind="ExternalOutput").ap()
    w_in = nc.dram_tensor("w_in", [D, 2 * D], F32, kind="ExternalInput").ap()
    lr_w = nc.dram_tensor("lr_w", [D, 128], F32, kind="ExternalInput").ap()
    w_dt = nc.dram_tensor("w_dt", [64, D], F32, kind="ExternalInput").ap()
    w_bp = nc.dram_tensor("w_bp", [16, D], F32, kind="ExternalInput").ap()
    w_cp = nc.dram_tensor("w_cp", [16, D], F32, kind="ExternalInput").ap()
    w_out = nc.dram_tensor("w_out", [D, D], F32, kind="ExternalInput").ap()
    w_delta = nc.dram_tensor("w_delta", [D, D], F32, kind="ExternalInput").ap()
    w_bc = nc.dram_tensor("w_bc", [D, 64], F32, kind="ExternalInput").ap()
    wg_T = nc.dram_tensor("wg_T", [D, INNER], F32, kind="ExternalInput").ap()
    wu_T = nc.dram_tensor("wu_T", [D, INNER], F32, kind="ExternalInput").ap()
    wd_T = nc.dram_tensor("wd_T", [INNER, D], F32, kind="ExternalInput").ap()
    vparams = nc.dram_tensor("vparams", [128, NV], F32, kind="ExternalInput").ap()
    cfg = nc.dram_tensor("cfg", [1, 8], I32, kind="ExternalInput").ap()
    ident_in = nc.dram_tensor("ident_in", [128, 128], F32, kind="ExternalInput").ap()
    ones_row = nc.dram_tensor("ones_row", [1, 128], F32, kind="ExternalInput").ap()

    w_in_d = nc.dram_tensor("w_in_d", [D, 2 * D], BF16)
    w_out_d = nc.dram_tensor("w_out_d", [D, D], BF16)
    w_delta_d = nc.dram_tensor("w_delta_d", [D, D], BF16)
    wgq_d = nc.dram_tensor("wgq_d", [D, INNER], BF16)
    wuq_d = nc.dram_tensor("wuq_d", [D, INNER], BF16)
    wdq_d = nc.dram_tensor("wdq_d", [INNER, D], BF16)
    bc_d = nc.dram_tensor("bc_d", [64, TC], BF16)
    ag_a_in = nc.dram_tensor("ag_a_in", [128, DC], F32)
    ag_a_out = nc.dram_tensor("ag_a_out", [2, 128, DC], F32)
    ag_b_in = nc.dram_tensor("ag_b_in", [128, 128], F32)
    ag_b_out = nc.dram_tensor("ag_b_out", [2, 128, 128], F32)

    S = nc.alloc_sbuf_tensor
    vp = S("vp", [128, NV], F32)
    xw = S("xw", [128, TT * D], F32)
    xb_buf = S("xb_buf", [128, DC * (TC + 2)], F32)
    sz = S("sz", [128, DC * TC], F32)
    r_buf = S("r_buf", [128, TT * D], F32)
    carryA = S("carryA", [128, DC], F32)
    carryB = S("carryB", [128, 128], F32)
    xbh_prev = S("xbh_prev", [128, DC], F32)
    ident_b = S("ident_b", [128, 128], BF16)
    ident_f = S("ident_f", [128, 128], F32)
    ones_f = S("ones_f", [1, 128], F32)
    ones_b = S("ones_b", [1, 128], BF16)
    ones3_b = S("ones3_b", [65, 128], BF16)
    cfg_t = S("cfg_t", [1, 8], I32)
    A_b = S("A_b", [128, DC * N16], F32)
    agoA = S("agoA", [128, 2 * DC], F32)
    agoB = S("agoB", [128, 2 * 128], F32)
    lr_b = S("lr_b", [128, DC * 128], BF16)
    wpc_b = S("wpc_b", [80, D], BF16)
    wcp_b2 = S("wcp_b2", [16, D], BF16)
    da1_cg = S("da1_cg", [16, TC], BF16)
    wbc_b = S("wbc_b", [128, DC * 64], BF16)
    da1_b = S("da1_b", [128, TC], BF16)
    colm = S("colm", [128, 64], F32)
    sumx = S("sumx", [128, TT], F32)
    sumsq = S("sumsq", [128, TT], F32)
    sln = S("sln", [128, 12 * TT], F32)
    hst = S("hst", [1, 16], F32)
    xnTh = S("xnTh", [128, DC], BF16)
    mcl = S("mcl", [128, 4], F32)
    qsc = S("qsc", [128, 32], F32)

    PF = 17952
    PB = 26624
    pf = S("pf", [128, PF], F32)
    pb = S("pb", [128, PB], BF16)
    pi = S("pi", [128, 1024], I8)
    dump = S("dump", [128, 1024], F32)

    x_act = pf[:, 0:4096]
    bsm = pf[:, 4096:12288]
    gu = pf[:, 0:16384]
    xn3 = pf[:, 16384:17408]
    gatebuf = pf[:, 17408:17920]
    slabA = pf[:, 0:SLAB]
    slabB = pf[:, SLAB:2 * SLAB]
    xnT_b = pf[:, 2 * SLAB:2 * SLAB + 4096]
    dsm = pf[:, 2 * SLAB + 4096:2 * SLAB + 4096 + 1536]
    tokmaj = pf[:, 13856:17952]
    pstg32 = pf[:, 0:4096]
    xhrow = pf[0:1, 0:1024]

    u_bf = pb[:, 0:4096]
    xnT_a = pb[:, 4096:8192]
    winbuf = [pb[:, 8192:10240], pb[:, 10240:12288]]
    xact_b = pb[:, 0:4096]
    g_bf = pb[:, 4096:8192]
    woutbuf = pb[:, 8192:12288]
    xi_bf = pb[:, 0:4096]
    xiT = pb[:, 4096:8192]
    wgubuf = [pb[:, 8192:12288], pb[:, 12288:16384],
              pb[:, 16384:20480], pb[:, 20480:24576]]
    q2stg = pb[:, 0:4096]
    q2stg2 = pb[:, 12288:16384]
    xiT2 = [pb[:, 4096:8192], pb[:, 8192:12288]]
    wd_buf = pb[:, 16384:24576]
    xnTb_bf = pb[:, 0:4096]
    bcstg = pb[:, 4096:4608]
    wdelbuf = pb[:, 4096:5120]
    bcsrcB = pb[:, 4096:7168]
    bcsrcC = pb[:, 7168:10240]
    bcastB = pb[:, 10240:18432]
    bcastC = pb[:, 18432:26624]
    pstg16 = pb[:, 0:4096]
    uh_b = pb[0:1, 12288:13312]

    ps_tr0 = nc.alloc_psum_tensor("ps_tr0", [128, 1024], BF16)
    ps_tr1 = nc.alloc_psum_tensor("ps_tr1", [128, 1024], BF16)
    ps_m = [nc.alloc_psum_tensor(f"ps_m{i}", [128, 512], F32) for i in range(2)]
    ps_x = [nc.alloc_psum_tensor(f"ps_x{i}", [128, 512], F32) for i in range(2)]
    ps_sm = nc.alloc_psum_tensor("ps_sm", [128, 512], F32)
    ps_h = nc.alloc_psum_tensor("ps_h", [128, 512], F32)

    assert nc.sbuf_bytes_remaining > 2048, nc.sbuf_bytes_remaining

    ENGS = ["sync", "scalar", "vector", "tensor", "gpsimd"]
    q = {e: [] for e in ENGS}
    sems = {}

    def mksem(name):
        sems[name] = [nc.alloc_semaphore(name), 0]
    for e in ENGS:
        mksem(e + "_c")
        mksem(e + "_d")
    for s in ["sCa", "sCb", "sAGa", "sAGb"]:
        mksem(s)
    regs = {}

    waited = {e: {} for e in ENGS}

    def emit(eng, fn, waits=(), inc=None, amt=1):
        wd = waited[eng]
        W = []
        for (s, v) in waits:
            if s is None:
                continue
            if wd.get(s, -1) >= v:
                continue
            wd[s] = v
            W.append((sems[s][0], v))
        H = None
        if inc:
            sems[inc][1] += amt
            H = sems[inc][0]

        def run(e, fn=fn, W=W, H=H, amt=amt):
            for h, v in W:
                e.wait_ge(h, v)
            r = fn(e)
            if H is not None:
                r.then_inc(H, amt)
        q[eng].append(run)

    def step(eng, fn, waits=()):
        s = eng + "_c"
        emit(eng, fn, waits=waits, inc=s, amt=1)
        return (s, sems[s][1])

    DSEMS = ["d_boot", "d_gboot", "d_stg", "d_gstg", "d_scr", "d_xw", "d_xh",
             "d_win0", "d_win1", "d_wout", "d_wgu0", "d_wgu1", "d_wd0", "d_wd1", "d_wdel",
             "d_bcs", "d_bcl", "d_ag", "d_out"]
    for s in DSEMS:
        mksem(s)

    def dstep(eng, fn, sem, waits=()):
        self_wait = (sem, sems[sem][1])
        emit(eng, fn, waits=list(waits) + [self_wait], inc=sem, amt=16)
        return (sem, sems[sem][1])

    def rwait(eng, regname, semname):
        def run(e, regname=regname, h=sems[semname][0]):
            e.wait_ge(h, regs[regname])
        q[eng].append(run)

    def vp_col(name, j=0):
        c0, w = VP.cols[name]
        return vp[:, c0 + j:c0 + j + 1]

    def vp_cols(name, j0, cnt):
        c0, w = VP.cols[name]
        return vp[:, c0 + j0:c0 + j0 + cnt]

    dstep("sync", lambda e: e.dma_start(vp[:], vparams), "d_boot")
    dstep("sync", lambda e: e.dma_start(cfg_t[:], cfg), "d_boot")
    dstep("sync", lambda e: e.dma_start(ident_f[:], ident_in), "d_boot")
    t_boot = dstep("sync", lambda e: e.dma_start(ones_f[:], ones_row), "d_boot")
    t_vp = t_cfg = t_id = t_on = t_boot

    t_idb = step("vector", lambda e: e.tensor_copy(ident_b[:], ident_f[:]), waits=[t_id])
    t_onb = step("vector", lambda e: e.tensor_copy(ones_b[:], ones_f[:]), waits=[t_on])
    for _rb in (0, 32, 64):
        t_gboot = dstep("gpsimd", lambda e, _rb=_rb: e.dma_start(ones3_b[_rb:_rb + 1, :], ones_row),
                        "d_gboot")
    t_lr = t_wpc = t_wbc = t_onb2 = t_gboot
    t_init = step("vector", lambda e: [e.memset(carryA[:], 0.0),
                                       e.memset(carryB[:], 0.0)][-1])
    t_ab1 = step("scalar", lambda e: e.activation(A_b[:], vp_cols("A_logb", 0, DC * N16), AF.Exp),
                 waits=[t_vp])
    t_ab = step("vector", lambda e: e.tensor_scalar_mul(A_b[:], A_b[:], -1.0), waits=[t_ab1])

    def reg_v(e):
        rWA = e.alloc_register("rWA"); e.load(rWA, cfg_t[0:1, 0:1])
        rWB = e.alloc_register("rWB"); e.load(rWB, cfg_t[0:1, 1:2])
        regs["vWA"], regs["vWB"] = rWA, rWB
    emit("vector", lambda e: (reg_v(e), e.engine_nop())[1], waits=[t_cfg])

    def reg_g(e):
        rKA = e.alloc_register("rKA"); e.load(rKA, cfg_t[0:1, 2:3])
        rKB = e.alloc_register("rKB"); e.load(rKB, cfg_t[0:1, 3:4])
        regs["gKA"], regs["gKB"] = rKA, rKB
    emit("gpsimd", lambda e: (reg_g(e), e.engine_nop())[1], waits=[t_cfg])

    dstep("gpsimd", lambda e: e.dma_start(lr_b[:], lr_w.rearrange("(c p) j -> p c j", p=128)), "d_gboot")
    dstep("gpsimd", lambda e: e.dma_start(wpc_b[0:64, :], w_dt), "d_gboot")
    dstep("gpsimd", lambda e: e.dma_start(wpc_b[64:80, :], w_bp), "d_gboot")
    dstep("gpsimd", lambda e: e.dma_start(wcp_b2[0:16, :], w_cp), "d_gboot")
    t_gboot = dstep("gpsimd", lambda e: e.dma_start(wbc_b[:], w_bc.rearrange("(c p) j -> p c j", p=128)), "d_gboot")
    t_lr = t_wpc = t_wbc = None

    def cast_through(src, dst, rows, cols, start_tok):
        prev = start_tok
        for i in range(rows // 128):
            tin = dstep("gpsimd", lambda e, i=i: e.dma_start(
                pstg16[:, 0:cols], src[i * 128:(i + 1) * 128, :]), "d_gstg", waits=[prev])
            prev = dstep("sync", lambda e, i=i: e.dma_start(
                dst[i * 128:(i + 1) * 128, :], pstg16[:, 0:cols]), "d_scr", waits=[tin])
        return prev

    t_win = cast_through(w_in, w_in_d.ap(), D, 2 * D, t_on)
    t_wout = cast_through(w_out, w_out_d.ap(), D, D, t_win)
    t_wdel = cast_through(w_delta, w_delta_d.ap(), D, D, t_wout)

    def quant_matrix(src, dst, rows, cols, mslot, start_tok):
        nt = rows // 128
        tacc = step("vector", lambda e: e.memset(colm[:, 0:nt], 0.0), waits=[start_tok])
        tprev = tacc
        tred = None
        for i in range(nt):
            tin = dstep("sync", lambda e, i=i: e.dma_start(
                pstg32[:, 0:cols], src[i * 128:(i + 1) * 128, :]), "d_stg",
                waits=[tprev, (tred[0], tred[1]) if tred else start_tok])
            tred = step("vector", lambda e, i=i: e.reduce_sum(
                colm[:, i:i + 1], pstg32[:, 0:cols], AX, apply_absolute_value=True), waits=[tin])
            tprev = tin
        t1 = step("vector", lambda e: e.reduce_sum(colm[:, 33:34], colm[:, 0:nt], AX), waits=[tred])
        t2 = step("tensor", lambda e: e.transpose(ps_sm[0:1, 0:128], colm[:, 33:34], ident_f[:]),
                  waits=[t1, t_id])
        t3 = step("vector", lambda e: e.reduce_sum(colm[0:1, 34:35], ps_sm[0:1, 0:128], AX), waits=[t2])
        t4 = step("vector", lambda e: e.tensor_scalar(colm[0:1, 35:36], colm[0:1, 34:35],
                                                      1.0 / (rows * cols), 1e-5, OP.mult, OP.max),
                  waits=[t3])
        t5 = step("vector", lambda e: e.reciprocal(colm[0:1, 36:37], colm[0:1, 35:36]), waits=[t4])
        t6 = step("tensor", lambda e: e.matmul(ps_sm[:, 128:130], ones_f[:], colm[0:1, 35:37],
                                               start=True, stop=True), waits=[t5, t_on, t3])
        t7 = step("vector", lambda e, mslot=mslot: e.tensor_copy(mcl[:, mslot:mslot + 1],
                                                                 ps_sm[:, 128:129]), waits=[t6])
        t8 = step("vector", lambda e, mslot=mslot: e.tensor_copy(qsc[:, 8 + mslot:9 + mslot],
                                                                 ps_sm[:, 129:130]), waits=[t6])
        tprev = t8
        for i in range(nt):
            tin = dstep("sync", lambda e, i=i: e.dma_start(
                pstg32[:, 0:cols], src[i * 128:(i + 1) * 128, :]), "d_stg", waits=[tprev])
            ta = step("vector", lambda e, mslot=mslot: e.tensor_scalar(
                pstg32[:, 0:cols], pstg32[:, 0:cols], qsc[:, 8 + mslot:9 + mslot], MAGIC,
                OP.mult, OP.add), waits=[tin])
            tb = step("vector", lambda e: e.tensor_scalar(
                pstg32[:, 0:cols], pstg32[:, 0:cols], -MAGIC, 1.0, OP.add, OP.min),
                waits=[ta])
            td = step("vector", lambda e: e.tensor_scalar_max(
                pstg16[:, 0:cols], pstg32[:, 0:cols], -1.0), waits=[tb])
            tprev = dstep("sync", lambda e, i=i: e.dma_start(
                dst[i * 128:(i + 1) * 128, :], pstg16[:, 0:cols]), "d_scr", waits=[td])
        return tprev

    if stages >= 2:
        t_qg = quant_matrix(wg_T, wgq_d.ap(), D, INNER, 0, t_wdel)
        t_qu = quant_matrix(wu_T, wuq_d.ap(), D, INNER, 1, t_qg)
        t_qd = quant_matrix(wd_T, wdq_d.ap(), INNER, D, 2, t_qu)
    else:
        t_qg = t_qu = t_qd = t_wdel
    t_scr = ("d_scr", sems["d_scr"][1])
    t_win = t_wout = t_wdel = t_qg = t_qu = t_qd = t_scr
    t_lr = t_wpc = t_wbc = t_gboot
    t_prologue = t_scr

    def ag_block(contrib_sem, reg_name, agi, ago_d, ago_t, carry_t, width, out_sem, init_tok):
        rwait("gpsimd", reg_name, contrib_sem)
        t1 = dstep("gpsimd", lambda e: e.dma_start(agi.ap(), carry_t[:]), "d_ag",
                   waits=[init_tok])
        emit("gpsimd", lambda e: e.collective_compute(
            "AllGather", OP.bypass,
            replica_groups=[[0, 1], [2, 3], [4, 5], [6, 7]],
            ins=[agi.ap().opt()],
            outs=[ago_d.ap().opt()],
        ), waits=[t1], inc="gpsimd_c", amt=1)
        tcc = ("gpsimd_c", sems["gpsimd_c"][1])
        t2 = dstep("gpsimd", lambda e: e.dma_start(
            ago_t[:].rearrange("p (s w) -> p s w", s=2),
            ago_d.ap().rearrange("s p w -> p s w")), "d_ag", waits=[tcc])
        emit("gpsimd", lambda e: e.tensor_scalar(
            carry_t[:], ago_t[:, 0:width], vp_col("msk"), None, OP.mult),
            waits=[t2, t_vp], inc=out_sem, amt=1)

    IDX = {"winbuf": 1, "wgubuf": 1, "xiT2": 1}

    def nxt(name):
        IDX[name] ^= 1
        return IDX[name]

    def ln_stats(tok_w, tok_sq, slot_s, slot_b, extra=None):
        t1 = step("vector", lambda e: e.tensor_scalar_mul(colm[:, 0:TT], sumx[:], 1.0 / D),
                  waits=[tok_w])
        t2 = step("vector", lambda e: e.tensor_scalar_mul(colm[:, 4:4 + TT], sumsq[:], 1.0 / D),
                  waits=[tok_sq])
        t3 = step("vector", lambda e: e.tensor_mul(colm[:, 8:8 + TT], colm[:, 0:TT], colm[:, 0:TT]),
                  waits=[t1])
        t4 = step("vector", lambda e: e.tensor_sub(colm[:, 12:12 + TT], colm[:, 4:4 + TT],
                                                   colm[:, 8:8 + TT]), waits=[t2, t3])
        t5 = step("scalar", lambda e: e.activation(colm[:, 16:16 + TT], colm[:, 12:12 + TT],
                                                   AF.Sqrt, bias=np.float32(EPS)), waits=[t4])
        t6 = step("vector", lambda e: e.reciprocal(colm[:, 20:20 + TT], colm[:, 16:16 + TT]),
                  waits=[t5])
        if extra is None:
            ta = step("vector", lambda e: e.tensor_copy(sln[:, slot_s * TT:(slot_s + 1) * TT],
                                                        colm[:, 20:20 + TT]), waits=[t6])
            tb = step("vector", lambda e: e.scalar_tensor_tensor(
                sln[:, slot_b * TT:(slot_b + 1) * TT], colm[:, 0:TT], -1.0,
                colm[:, 20:20 + TT], OP.mult, OP.mult), waits=[t6])
            return [ta, tb]
        t7 = step("vector", lambda e: e.tensor_mul(colm[:, 24:24 + TT], colm[:, 20:20 + TT],
                                                   colm[:, 20:20 + TT]), waits=[t6])
        t8 = step("vector", lambda e: e.tensor_mul(colm[:, 28:28 + TT], colm[:, 24:24 + TT],
                                                   colm[:, 12:12 + TT]), waits=[t7])
        t9 = step("scalar", lambda e: e.activation(colm[:, 32:32 + TT], colm[:, 28:28 + TT],
                                                   AF.Sqrt, bias=np.float32(EPS)), waits=[t8])
        t10 = step("vector", lambda e: e.reciprocal(colm[:, 36:36 + TT], colm[:, 32:32 + TT]),
                   waits=[t9])
        if extra == "attn_a":
            s_r, b_r, s_u, b_u = 0, 1, 2, 3
        else:
            s_r, b_r, s_u, b_u = 8, 9, 6, 7
        ta = step("vector", lambda e, s_r=s_r: e.tensor_scalar_add(
            sln[:, s_r * TT:(s_r + 1) * TT], colm[:, 20:20 + TT], 1.0), waits=[t6])
        tb = step("vector", lambda e, b_r=b_r: e.scalar_tensor_tensor(
            sln[:, b_r * TT:(b_r + 1) * TT], colm[:, 0:TT], -1.0, colm[:, 20:20 + TT],
            OP.mult, OP.mult), waits=[t6])
        tc = step("vector", lambda e, s_u=s_u: e.tensor_mul(
            sln[:, s_u * TT:(s_u + 1) * TT], colm[:, 20:20 + TT], colm[:, 36:36 + TT]),
            waits=[t10])
        td = step("vector", lambda e, b_u=b_u, b_r=b_r: e.tensor_mul(
            sln[:, b_u * TT:(b_u + 1) * TT], sln[:, b_r * TT:(b_r + 1) * TT],
            colm[:, 36:36 + TT]), waits=[tb, t10])
        return [ta, tb, tc, td]

    def halo_token(row_idx, waits=()):
        tin = dstep("sync", lambda e: e.dma_start(xhrow[:], x_sh[row_idx:row_idx + 1, :]),
                    "d_xh", waits=list(waits))
        t1 = step("vector", lambda e: e.reduce_sum(hst[0:1, 0:1], xhrow[:], AX), waits=[tin])
        t2 = step("scalar", lambda e: e.activation(dump[0:1, :], xhrow[:], AF.Square,
                                                   accum_out=hst[0:1, 1:2]), waits=[tin])
        t3a = step("vector", lambda e: e.tensor_scalar_mul(hst[0:1, 2:3], hst[0:1, 0:1], 1.0 / D),
                   waits=[t1])
        t3b = step("vector", lambda e: e.tensor_scalar_mul(hst[0:1, 3:4], hst[0:1, 1:2], 1.0 / D),
                   waits=[t2])
        t3c = step("vector", lambda e: e.tensor_mul(hst[0:1, 4:5], hst[0:1, 2:3], hst[0:1, 2:3]),
                   waits=[t3a])
        t3 = step("vector", lambda e: e.tensor_sub(hst[0:1, 5:6], hst[0:1, 3:4], hst[0:1, 4:5]),
                  waits=[t3b, t3c])
        t4 = step("scalar", lambda e: e.activation(hst[0:1, 6:7], hst[0:1, 5:6], AF.Sqrt,
                                                   bias=np.float32(EPS)), waits=[t3])
        t5 = step("vector", lambda e: e.reciprocal(hst[0:1, 7:8], hst[0:1, 6:7]), waits=[t4])
        t6a = step("vector", lambda e: e.tensor_mul(hst[0:1, 8:9], hst[0:1, 7:8], hst[0:1, 7:8]),
                   waits=[t5])
        t6 = step("vector", lambda e: e.tensor_mul(hst[0:1, 9:10], hst[0:1, 8:9], hst[0:1, 5:6]),
                  waits=[t6a])
        t7 = step("scalar", lambda e: e.activation(hst[0:1, 10:11], hst[0:1, 9:10], AF.Sqrt,
                                                   bias=np.float32(EPS)), waits=[t6])
        t8 = step("vector", lambda e: e.reciprocal(hst[0:1, 11:12], hst[0:1, 10:11]), waits=[t7])
        t9a = step("vector", lambda e: e.tensor_mul(hst[0:1, 12:13], hst[0:1, 7:8], hst[0:1, 11:12]),
                   waits=[t8])
        t9 = step("vector", lambda e: e.scalar_tensor_tensor(
            hst[0:1, 13:14], hst[0:1, 2:3], -1.0, hst[0:1, 12:13], OP.mult, OP.mult),
            waits=[t9a])
        t10 = step("scalar", lambda e: e.activation(uh_b[:], xhrow[:], AF.Identity,
                                                    bias=hst[0:1, 13:14], scale=hst[0:1, 12:13]),
                   waits=[t9])
        ttrs = []
        for c in range(DC):
            ttrs.append(step("tensor", lambda e, c=c: e.transpose(
                ps_tr1[:, 512 + 2 * c:512 + 2 * c + 1], uh_b[0:1, c * 128:(c + 1) * 128],
                ident_b[0:1, 0:1]), waits=[t10, t_idb]))
        evs = []
        for c in range(DC):
            evs.append(step("vector", lambda e, c=c: e.tensor_scalar(
                xnTh[:, c:c + 1], ps_tr1[:, 512 + 2 * c:512 + 2 * c + 1],
                vp_col("gamma_a", c), vp_col("beta_a", c), OP.mult, OP.add),
                waits=[ttrs[-1], t_vp]))
        return evs[-1]

    def stage_A_pre(k, gate):
        r0 = k * TC + 1
        t_x = dstep("sync", lambda e, r0=r0: e.dma_start(
            xw[:].rearrange("p (tt d) -> p tt d", tt=TT),
            x_sh[r0:r0 + TC, :].rearrange("(tt p) d -> p tt d", p=128)), "d_xw",
            waits=list(gate))
        tsx = tsq = None
        for tt in range(TT):
            tsx = step("vector", lambda e, tt=tt: e.reduce_sum(
                sumx[:, tt:tt + 1], xw[:, tt * D:(tt + 1) * D], AX), waits=[t_x])
            tsq = step("scalar", lambda e, tt=tt: e.activation(
                dump[:], xw[:, tt * D:(tt + 1) * D], AF.Square,
                accum_out=sumsq[:, tt:tt + 1]),
                waits=[t_x] + ([tsq] if tsq else list(gate)))
        toks = ln_stats(tsx, tsq, None, None, extra="attn_a")
        return {"tsx": tsx, "tsq": tsq, "t_ln": toks[-1], "t_ln2": toks[-2]}

    def stage_A(k, t_entry, pre=None):
        if pre is None:
            pre = stage_A_pre(k, list(t_entry))
        tsx, tsq = pre["tsx"], pre["tsq"]
        t_ln, t_ln2 = pre["t_ln"], pre["t_ln2"]
        if DBG_CUT == 31:
            return tsx, tsq, tsq, tsq
        if DBG_CUT == 32:
            return t_ln, t_ln2, t_ln2, t_ln2
        tu = []
        for tt in range(TT):
            tu.append(step("scalar", lambda e, tt=tt: e.activation(
                u_bf[:, tt * D:(tt + 1) * D], xw[:, tt * D:(tt + 1) * D], AF.Identity,
                bias=sln[:, 3 * TT + tt:3 * TT + tt + 1],
                scale=sln[:, 2 * TT + tt:2 * TT + tt + 1]),
                waits=[t_ln, t_ln2] + (list(t_entry) if tt == 0 else [])))
        if DBG_CUT == 33:
            return tu[-1], tu[-2], tu[-2], tu[-2]
        th = halo_token((k + 1) * TC + 1, waits=[tu[-1]])
        if DBG_CUT == 34:
            return th, tu[-1], tu[-1], tu[-1]
        tevs = []
        for tt in range(TT):
            pst = ps_tr0 if tt % 2 == 0 else ps_tr1
            ttr = []
            for c in range(DC):
                w = [tu[tt], t_idb]
                if c == 0:
                    w.append(th)
                if len(tevs) >= 9:
                    w.append(tevs[-9])
                ttr.append(step("tensor", lambda e, tt=tt, c=c, pst=pst: e.transpose(
                    pst[:, c * 128:(c + 1) * 128],
                    u_bf[:, tt * D + c * 128:tt * D + (c + 1) * 128], ident_b[:]), waits=w))
            for c in range(DC):
                tevs.append(step("vector", lambda e, tt=tt, c=c, pst=pst: e.tensor_scalar(
                    xnT_a[:, c * TC + tt * 128:c * TC + (tt + 1) * 128],
                    pst[:, c * 128:(c + 1) * 128],
                    vp_col("gamma_a", c), vp_col("beta_a", c), OP.mult, OP.add),
                    waits=[ttr[-1], t_vp]))
        t_xnT = tevs[-1]
        if DBG_CUT == 35:
            return t_xnT, tevs[-2], tevs[-2], tevs[-2]
        ev_toks = []
        halo_done = None
        halo_toks = {}
        for dp in range(8):
            slot = nxt("winbuf")
            wb = winbuf[slot]
            w = [t_win] + list(t_entry)
            if len(ev_toks) >= 3:
                w.append(ev_toks[-3])
            if dp - 2 in halo_toks:
                w.append(halo_toks[dp - 2])
            t_ws = dstep("sync", lambda e, dp=dp, wb=wb: e.dma_start(
                wb[:].rearrange("p (c n) -> p c n", c=DC),
                w_in_d.ap().rearrange("(c p) n -> p c n", p=128)[:, :, dp * 256:(dp + 1) * 256]),
                "d_win0" if slot == 0 else "d_win1", waits=w)
            mm_t = None
            for c in range(DC):
                w0 = [t_ws, t_xnT]
                if len(ev_toks) >= 2 and c == 0:
                    w0.append(ev_toks[-2])
                step("tensor", lambda e, c=c, wb=wb: e.matmul(
                    ps_m[0][:], wb[:, c * 256:c * 256 + 128],
                    xnT_a[:, c * TC:(c + 1) * TC], start=(c == 0), stop=(c == DC - 1)), waits=w0)
                mm_t = step("tensor", lambda e, c=c, wb=wb: e.matmul(
                    ps_m[1][:], wb[:, c * 256 + 128:(c + 1) * 256],
                    xnT_a[:, c * TC:(c + 1) * TC], start=(c == 0), stop=(c == DC - 1)),
                    waits=[ev_toks[-1]] if (c == 0 and len(ev_toks) >= 1) else [])
            if dp < 4:
                for j in range(2):
                    for c in range(DC):
                        wh = [t_ws] if c == 0 else []
                        if c == 0 and j == 0:
                            wh.append(th)
                        halo_done = step("tensor", lambda e, dp=dp, c=c, j=j, wb=wb: e.matmul(
                            ps_h[:, 2 * dp + j:2 * dp + j + 1],
                            wb[:, c * 256 + j * 128:c * 256 + (j + 1) * 128],
                            xnTh[:, c:c + 1], start=(c == 0), stop=(c == DC - 1)), waits=wh)
                halo_toks[dp] = halo_done
            for j in range(2):
                do = 2 * dp + j
                if do < DC:
                    ev_toks.append(step("scalar", lambda e, do=do, j=j: e.activation(
                        xb_buf[:, do * (TC + 2) + 1:do * (TC + 2) + 1 + TC], ps_m[j][:],
                        AF.Identity), waits=[mm_t]))
                else:
                    c2 = do - DC
                    ev_toks.append(step("scalar", lambda e, c2=c2, j=j: e.activation(
                        sz[:, c2 * TC:(c2 + 1) * TC], ps_m[j][:], AF.Silu), waits=[mm_t]))
        mask_ap = vp_col("hm", 1) if k == NCH - 1 else vp_col("onescol")
        t_lh = step("vector", lambda e: e.tensor_copy(
            xb_buf[:].rearrange("p (c w) -> p c w", w=TC + 2)[:, :, 0], xbh_prev[:]),
            waits=list(t_entry) + [ev_toks[0]])
        t_hev = step("vector", lambda e, mask_ap=mask_ap: e.tensor_scalar(
            xb_buf[:].rearrange("p (c w) -> p c w", w=TC + 2)[:, :, TC + 1],
            ps_h[:, 0:DC], mask_ap, None, OP.mult), waits=[halo_done, t_vp])
        t_sv = step("vector", lambda e: e.tensor_copy(
            xbh_prev[:], xb_buf[:].rearrange("p (c w) -> p c w", w=TC + 2)[:, :, TC + 1]),
            waits=[t_hev])
        return ev_toks[-1], t_hev, t_lh, t_sv

    def stage_B(k, t_A, t_hev, t_lh, t_outprev):
        t_r1_last = None
        W = TC + 2
        t_conv = []
        for c in range(DC):
            b = xb_buf[:, c * W:(c + 1) * W]
            cs = 3072 + (c % 2) * 512
            t1 = step("vector", lambda e, c=c, b=b, cs=cs: e.tensor_scalar(
                bsm[:, cs:cs + TC], b[:, 0:TC], vp_col("conv_w", 3 * c + 0), None, OP.mult),
                waits=[t_A, t_hev, t_lh, t_vp] + ([t_conv[-2]] if len(t_conv) >= 2 else []))
            t2 = step("vector", lambda e, c=c, b=b, cs=cs: e.scalar_tensor_tensor(
                bsm[:, cs:cs + TC], b[:, 1:1 + TC], vp_col("conv_w", 3 * c + 1),
                bsm[:, cs:cs + TC], OP.mult, OP.add), waits=[t1])
            t3 = step("vector", lambda e, c=c, b=b, cs=cs: e.scalar_tensor_tensor(
                bsm[:, cs:cs + TC], b[:, 2:2 + TC], vp_col("conv_w", 3 * c + 2),
                bsm[:, cs:cs + TC], OP.mult, OP.add), waits=[t2])
            t_conv.append(step("scalar", lambda e, c=c, cs=cs: e.activation(
                x_act[:, c * TC:(c + 1) * TC], bsm[:, cs:cs + TC], AF.Silu,
                bias=vp_col("conv_b", c)), waits=[t3]))
        t_xab = step("vector", lambda e: e.tensor_copy(xact_b[:], x_act[:]), waits=[t_conv[-1]])
        mm = None
        for c in range(DC):
            step("tensor", lambda e, c=c: e.matmul(
                ps_sm[0:80, :], lr_b[:, c * 128:c * 128 + 80], xact_b[:, c * TC:(c + 1) * TC],
                start=(c == 0), stop=(c == DC - 1)), waits=[t_xab, t_lr])
            mm = step("tensor", lambda e, c=c: e.matmul(
                ps_h[0:16, 0:TC], lr_b[:, c * 128 + 96:c * 128 + 112],
                xact_b[:, c * TC:(c + 1) * TC],
                start=(c == 0), stop=(c == DC - 1)), waits=[])
        t_da1a = step("vector", lambda e: e.tensor_copy(da1_b[0:80, :], ps_sm[0:80, :]), waits=[mm])
        t_da1 = step("vector", lambda e: e.tensor_copy(da1_cg[0:16, :], ps_h[0:16, 0:TC]),
                     waits=[t_da1a])
        gate_last = None
        pe_ev = {0: None, 1: None, 2: None}
        prev = {}
        for c in range(DC):
            o = 4096 if c % 2 else 0
            S0, S1, S2, S3, S4 = o, o + TC, o + 2 * TC, o + 3 * TC, o + 4 * TC
            w0 = [t_da1, t_wpc]
            t1 = step("tensor", lambda e, c=c: e.matmul(
                ps_m[0][:], wpc_b[0:64, c * 128:(c + 1) * 128], da1_b[0:64, :],
                start=True, stop=True), waits=w0 + ([pe_ev[0]] if pe_ev[0] else []))
            t2 = step("scalar", lambda e, c=c: e.activation(dump[:, 0:TC], ps_m[0][:], AF.Exp,
                                                            bias=vp_col("b_dt", c)),
                      waits=[t1] + ([prev.get("t3")] if prev.get("t3") else []))
            t3 = step("scalar", lambda e, c=c, S0=S0: e.activation(
                bsm[:, S0:S0 + TC], dump[:, 0:TC], AF.Ln,
                bias=np.float32(1.0)), waits=[t2])
            pe_ev[0] = t2
            prev["t3"] = t3
            t4 = step("tensor", lambda e, c=c: e.matmul(
                ps_m[1][:], wpc_b[64:80, c * 128:(c + 1) * 128], da1_b[64:80, :],
                start=True, stop=True), waits=w0 + ([pe_ev[1]] if pe_ev[1] else []))
            t5 = step("vector", lambda e, c=c, S1=S1: e.tensor_copy(
                bsm[:, S1:S1 + TC], ps_m[1][:]), waits=[t4])
            pe_ev[1] = t5
            t6 = step("tensor", lambda e, c=c: e.matmul(
                ps_x[0][:], wcp_b2[0:16, c * 128:(c + 1) * 128], da1_cg[0:16, :],
                start=True, stop=True), waits=w0 + ([pe_ev[2]] if pe_ev[2] else []))
            t7 = step("vector", lambda e, c=c, S2=S2: e.tensor_copy(
                bsm[:, S2:S2 + TC], ps_x[0][:]), waits=[t6])
            pe_ev[2] = t7
            t8 = step("vector", lambda e, c=c, S0=S0, S3=S3: e.tensor_mul(
                bsm[:, S3:S3 + TC], bsm[:, S0:S0 + TC], x_act[:, c * TC:(c + 1) * TC]),
                waits=[t3])
            t9 = step("scalar", lambda e, c=c, S0=S0: e.activation(
                bsm[:, S0:S0 + TC], bsm[:, S0:S0 + TC], AF.Exp, scale=vp_col("A_a", c)),
                waits=[t8, t_ab])
            t10 = step("vector", lambda e, c=c, S1=S1, S3=S3: e.tensor_mul(
                bsm[:, S1:S1 + TC], bsm[:, S3:S3 + TC], bsm[:, S1:S1 + TC]), waits=[t9, t5])
            if k == 0 and c == 0:
                rwait("vector", "vWA", "sAGa")
            t11 = step("vector", lambda e, c=c, S0=S0, S1=S1, S4=S4: e.tensor_tensor_scan(
                bsm[:, S4:S4 + TC], bsm[:, S0:S0 + TC], bsm[:, S1:S1 + TC],
                carryA[:, c:c + 1], OP.mult, OP.add),
                waits=[t10] + ([t_init] if k == 0 and c == 0 else [])
                + ([prev.get("t11")] if prev.get("t11") else []))
            prev["t11"] = t11
            t12 = step("vector", lambda e, c=c, S4=S4: e.tensor_copy(
                carryA[:, c:c + 1], bsm[:, S4 + TC - 1:S4 + TC]), waits=[t11])
            if k == min(NCH, DBG_CHUNKS) - 1 and c == DC - 1:
                emit("vector", lambda e: e.engine_nop(), waits=[t12], inc="sCa", amt=1)
            t13 = step("vector", lambda e, c=c, S2=S2, S4=S4: e.tensor_mul(
                bsm[:, S2:S2 + TC], bsm[:, S2:S2 + TC], bsm[:, S4:S4 + TC]),
                waits=[t11, t7])
            t14 = step("vector", lambda e, c=c, S2=S2: e.scalar_tensor_tensor(
                bsm[:, S2:S2 + TC], x_act[:, c * TC:(c + 1) * TC], vp_col("Dsk_a", c),
                bsm[:, S2:S2 + TC], OP.mult, OP.add), waits=[t13])
            t15 = step("vector", lambda e, c=c, S2=S2: e.tensor_mul(
                bsm[:, S2:S2 + TC], bsm[:, S2:S2 + TC], sz[:, c * TC:(c + 1) * TC]),
                waits=[t14])
            gate_last = step("vector", lambda e, c=c, S2=S2: e.tensor_copy(
                g_bf[:, c * TC:(c + 1) * TC], bsm[:, S2:S2 + TC]), waits=[t15])
        ev = []
        for dh in range(2):
            w = [t_wout, t_A]
            if len(ev) >= 2:
                w.append(ev[-1])
            t_w = dstep("sync", lambda e, dh=dh: e.dma_start(
                woutbuf[:].rearrange("p (c n) -> p c n", c=DC),
                w_out_d.ap().rearrange("(c p) n -> p c n", p=128)[:, :, dh * 512:(dh + 1) * 512]),
                "d_wout", waits=w)
            for tt in range(TT):
                mm = None
                for c in range(DC):
                    w0 = [t_w, gate_last]
                    if len(ev) >= 2 and c == 0:
                        w0.append(ev[-2])
                    mm = step("tensor", lambda e, tt=tt, c=c: e.matmul(
                        ps_m[tt % 2][:], g_bf[:, c * TC + tt * 128:c * TC + (tt + 1) * 128],
                        woutbuf[:, c * 512:(c + 1) * 512], start=(c == 0), stop=(c == DC - 1)),
                        waits=w0)
                t_r1 = step("scalar", lambda e, tt=tt, dh=dh: e.activation(
                    r_buf[:, tt * D + dh * 512:tt * D + (dh + 1) * 512],
                    xw[:, tt * D + dh * 512:tt * D + (dh + 1) * 512], AF.Identity,
                    bias=sln[:, 1 * TT + tt:1 * TT + tt + 1],
                    scale=sln[:, 0 * TT + tt:0 * TT + tt + 1]),
                    waits=[mm] + ([t_outprev] if t_outprev else []))
                t_r1_last = t_r1
                ev.append(step("vector", lambda e, tt=tt, dh=dh: e.scalar_tensor_tensor(
                    r_buf[:, tt * D + dh * 512:tt * D + (dh + 1) * 512], ps_m[tt % 2][:], 1.0,
                    r_buf[:, tt * D + dh * 512:tt * D + (dh + 1) * 512], OP.mult, OP.add,
                    accum_out=colm[:, 40 + tt * 2 + dh:41 + tt * 2 + dh]), waits=[t_r1]))
        t_sx = step("vector", lambda e: e.tensor_add(
            sumx[:], colm[:].rearrange("p (a b) -> p a b", a=32, b=2)[:, 20:24, 0],
            colm[:].rearrange("p (a b) -> p a b", a=32, b=2)[:, 20:24, 1]), waits=[ev[-1]])
        tsq = None
        for tt in range(TT):
            tsq = step("scalar", lambda e, tt=tt: e.activation(
                dump[:], r_buf[:, tt * D:(tt + 1) * D], AF.Square,
                accum_out=sumsq[:, tt:tt + 1]), waits=[ev[-1]] + ([tsq] if tsq else []))
        return t_sx, tsq, t_r1_last

    def stage_FFN(k, t_sx, t_sq, slot_s, slot_b, stats=True, tail_cb=None):
        toks = ln_stats(t_sx, t_sq, slot_s, slot_b, extra=None)
        t_ln = toks[-1]
        t_ln2 = toks[-2]
        t_xi = None
        t_coef = None
        t6_hist = {}
        txi_hist = {}
        for tt in range(TT):
            sc2 = 0 if tt % 2 == 0 else 28
            xnb = xn3 if tt % 2 == 0 else pf[:, 0:1024]
            w1 = [t_ln, t_ln2]
            if tt - 2 in t6_hist:
                w1.append(t6_hist[tt - 2])
            t1 = step("scalar", lambda e, tt=tt, xnb=xnb: e.activation(
                xnb, r_buf[:, tt * D:(tt + 1) * D], AF.Identity,
                bias=sln[:, slot_b * TT + tt:slot_b * TT + tt + 1],
                scale=sln[:, slot_s * TT + tt:slot_s * TT + tt + 1]), waits=w1)
            t2 = step("vector", lambda e, sc2=sc2, xnb=xnb: e.reduce_max(
                qsc[:, sc2:sc2 + 1], xnb, AX, apply_absolute_value=True), waits=[t1])
            t3 = step("vector", lambda e, sc2=sc2: [
                e.tensor_scalar_max(qsc[:, sc2 + 1:sc2 + 2], qsc[:, sc2:sc2 + 1], 1e-5)][-1],
                waits=[t2])
            t4 = step("vector", lambda e, sc2=sc2: e.reciprocal(
                qsc[:, sc2 + 2:sc2 + 3], qsc[:, sc2 + 1:sc2 + 2]), waits=[t3])
            t5 = step("vector", lambda e, sc2=sc2: e.tensor_scalar_mul(
                qsc[:, sc2 + 3:sc2 + 4], qsc[:, sc2 + 2:sc2 + 3], 127.0), waits=[t4])
            t6 = step("scalar", lambda e, sc2=sc2, xnb=xnb: e.activation(
                dump[:], xnb, AF.Identity, bias=np.float32(MAGIC),
                scale=qsc[:, sc2 + 3:sc2 + 4]),
                waits=[t5] + ([txi_hist[tt - 1]] if tt - 1 in txi_hist else []))
            t6_hist[tt] = t6
            t_xi = step("scalar", lambda e, tt=tt: e.activation(
                xi_bf[:, tt * D:(tt + 1) * D], dump[:], AF.Identity,
                bias=np.float32(-MAGIC)), waits=[t6])
            txi_hist[tt] = t_xi
            t7 = step("vector", lambda e, tt=tt, sc2=sc2: e.tensor_scalar_mul(
                qsc[:, 4 + tt:5 + tt], qsc[:, sc2 + 1:sc2 + 2], 1.0 / 127.0), waits=[t3])
            t8 = step("vector", lambda e, tt=tt: e.tensor_mul(
                qsc[:, 16 + tt:17 + tt], qsc[:, 4 + tt:5 + tt], mcl[:, 0:1]),
                waits=[t7, t_qg])
            t_coef = step("vector", lambda e, tt=tt: e.tensor_mul(
                qsc[:, 20 + tt:21 + tt], qsc[:, 4 + tt:5 + tt], mcl[:, 1:2]),
                waits=[t7, t_qu])
        tev = []
        for tt in range(TT):
            pst = ps_tr0 if tt % 2 == 0 else ps_tr1
            ttr = []
            for c in range(DC):
                w = [t_xi, t_idb]
                if len(tev) >= 9:
                    w.append(tev[-9])
                ttr.append(step("tensor", lambda e, tt=tt, c=c, pst=pst: e.transpose(
                    pst[:, c * 128:(c + 1) * 128],
                    xi_bf[:, tt * D + c * 128:tt * D + (c + 1) * 128], ident_b[:]), waits=w))
            for c in range(DC):
                tev.append(step("vector", lambda e, tt=tt, c=c, pst=pst: e.tensor_copy(
                    xiT[:, c * TC + tt * 128:c * TC + (tt + 1) * 128],
                    pst[:, c * 128:(c + 1) * 128]), waits=[ttr[-1]]))
        t_xiT = tev[-1]
        t_dbg = None
        if DBG_DUMP and k == 0 and slot_s == 4:
            t_dbg = dstep("gpsimd", lambda e: e.dma_start(dbg_xi, xi_bf[:]), "d_ag", waits=[t_xiT])
            dstep("sync", lambda e: e.dma_start(dbg_r2, r_buf[:]), "d_out", waits=[t_sx])
        t_gu = []
        tg_hist = []
        mm_g_by_io = {}
        mm_u_by_io = {}
        gi = 0
        for io in range(8):
            p2 = io % 2
            wbg = wgubuf[2 * p2]
            w = [t_qg, t_qu, t_sx]
            if io - 2 in mm_g_by_io:
                w.append(mm_g_by_io[io - 2])
            t_wga = dstep("sync", lambda e, io=io, wbg=wbg: e.dma_start(
                wbg[:, 0:2048].rearrange("p (c n) -> p c n", c=4),
                wgq_d.ap().rearrange("(c p) n -> p c n", p=128)
                [:, 0:4, io * 512:(io + 1) * 512]), "d_wgu0", waits=w)
            t_wgb = dstep("sync", lambda e, io=io, wbg=wbg: e.dma_start(
                wbg[:, 2048:4096].rearrange("p (c n) -> p c n", c=4),
                wgq_d.ap().rearrange("(c p) n -> p c n", p=128)
                [:, 4:8, io * 512:(io + 1) * 512]), "d_wd0", waits=list(w))
            wbu = wgubuf[2 * p2 + 1]
            wu = [t_qu, t_sx] + ([mm_u_by_io[io - 2]] if io - 2 in mm_u_by_io else [])
            t_wua = dstep("sync", lambda e, io=io, wbu=wbu: e.dma_start(
                wbu[:, 0:2048].rearrange("p (c n) -> p c n", c=4),
                wuq_d.ap().rearrange("(c p) n -> p c n", p=128)
                [:, 0:4, io * 512:(io + 1) * 512]), "d_wgu1", waits=wu)
            t_wub = dstep("sync", lambda e, io=io, wbu=wbu: e.dma_start(
                wbu[:, 2048:4096].rearrange("p (c n) -> p c n", c=4),
                wuq_d.ap().rearrange("(c p) n -> p c n", p=128)
                [:, 4:8, io * 512:(io + 1) * 512]), "d_wd1", waits=list(wu))
            for tt in range(TT):
                p = gi % 2
                gb = gatebuf[:, 0:512] if p == 0 else xn3[:, 0:512]
                mmg = None
                for c in range(DC):
                    w0 = [t_wga if c < 4 else t_wgb, t_xiT]
                    if c == 0 and len(tg_hist) >= 2:
                        w0.append(tg_hist[-2])
                    mmg = step("tensor", lambda e, tt=tt, c=c, p=p, wbg=wbg: e.matmul(
                        ps_m[p][:], xiT[:, c * TC + tt * 128:c * TC + (tt + 1) * 128],
                        wbg[:, c * 512:(c + 1) * 512], start=(c == 0), stop=(c == DC - 1)),
                        waits=w0)
                tg = step("scalar", lambda e, tt=tt, p=p, gb=gb: e.activation(
                    gb, ps_m[p][:], AF.Silu, scale=qsc[:, 16 + tt:17 + tt]),
                    waits=[mmg, t_coef] + ([t_gu[-2]] if len(t_gu) >= 2 else []))
                tg_hist.append(tg)
                mmu = None
                for c in range(DC):
                    mmu = step("tensor", lambda e, tt=tt, c=c, p=p, wbu=wbu: e.matmul(
                        ps_x[p][:], xiT[:, c * TC + tt * 128:c * TC + (tt + 1) * 128],
                        wbu[:, c * 512:(c + 1) * 512], start=(c == 0), stop=(c == DC - 1)),
                        waits=[t_wua if c < 4 else t_wub]
                        + ([t_gu[-2]] if c == 0 and len(t_gu) >= 2 else []))
                t_gu.append(step("vector", lambda e, tt=tt, io=io, p=p, gb=gb:
                                 e.scalar_tensor_tensor(
                    gu[:, tt * INNER + io * 512:tt * INNER + (io + 1) * 512], ps_x[p][:],
                    qsc[:, 20 + tt:21 + tt], gb, OP.mult, OP.mult),
                    waits=[mmu, tg]))
                if tt == TT - 1:
                    mm_g_by_io[io] = mmg
                    mm_u_by_io[io] = mmu
                gi += 1
        t_dbg2 = None
        if DBG_DUMP and k == 0 and slot_s == 4:
            t_dbg2 = dstep("sync", lambda e: e.dma_start(dbg_gu, gu[:]), "d_out", waits=[t_gu[-1]])
        ev2 = []
        wd_last = {0: None, 1: None}
        tq_hist = {}
        ttr_hist = {}
        tx2_hist = {}
        t_cds = {}
        t_x2s = {}
        dn_bank = {(0, 0): ps_m[0], (1, 0): ps_m[1], (0, 1): ps_x[0], (1, 1): ps_x[1]}
        ev_by_bank = {}
        wds = 0

        def quant2_part(tt):
            sc = 0 if tt % 2 == 0 else 12
            q2 = q2stg if tt % 2 == 0 else q2stg2
            w1 = [t_gu[-1]]
            if tt - 2 in tq_hist:
                w1.append(tq_hist[tt - 2][-1])
            t1 = step("vector", lambda e, tt=tt, sc=sc: e.reduce_max(
                qsc[:, sc:sc + 1], gu[:, tt * INNER:(tt + 1) * INNER], AX,
                apply_absolute_value=True), waits=w1)
            t2 = step("vector", lambda e, sc=sc: e.tensor_scalar_max(
                qsc[:, sc + 1:sc + 2], qsc[:, sc:sc + 1], 1e-5), waits=[t1])
            t3 = step("vector", lambda e, sc=sc: e.reciprocal(
                qsc[:, sc + 2:sc + 3], qsc[:, sc + 1:sc + 2]), waits=[t2])
            t4 = step("vector", lambda e, sc=sc: e.tensor_scalar_mul(
                qsc[:, sc + 3:sc + 4], qsc[:, sc + 2:sc + 3], 127.0), waits=[t3])
            t5 = step("vector", lambda e, tt=tt, sc=sc: e.tensor_scalar_mul(
                qsc[:, 4 + tt:5 + tt], qsc[:, sc + 1:sc + 2], 1.0 / 127.0), waits=[t2])
            t_cds[tt] = step("vector", lambda e, tt=tt: e.tensor_mul(
                qsc[:, 24 + tt:25 + tt], qsc[:, 4 + tt:5 + tt], mcl[:, 2:3]),
                waits=[t5, t_qd])
            tq = []
            for ih in range(4):
                wa = [t4] + ([tq[-1]] if tq else [])
                if ih == 0 and tt - 1 in tq_hist:
                    wa.append(tq_hist[tt - 1][-1])
                if ih == 0 and tt - 2 in ttr_hist:
                    wa.append(ttr_hist[tt - 2])
                if ih == 0 and tt == 1:
                    wa.append(mm_u_by_io[6])
                if t_dbg is not None and tt == 0 and ih == 0:
                    wa += [t_dbg, t_dbg2]
                ta = step("scalar", lambda e, tt=tt, ih=ih, sc=sc: e.activation(
                    dump[:],
                    gu[:, tt * INNER + ih * 1024:tt * INNER + (ih + 1) * 1024],
                    AF.Identity, bias=np.float32(MAGIC), scale=qsc[:, sc + 3:sc + 4]),
                    waits=wa)
                tq.append(step("scalar", lambda e, tt=tt, ih=ih, q2=q2: e.activation(
                    q2[:, ih * 1024:(ih + 1) * 1024], dump[:],
                    AF.Identity, bias=np.float32(-MAGIC)), waits=[ta]))
            tq_hist[tt] = tq
            return tq

        def transpose_part(tt):
            tq = tq_hist[tt]
            q2 = q2stg if tt % 2 == 0 else q2stg2
            xt2 = xiT2[tt % 2]
            ttr = None
            tx2 = []
            for ci in range(IC):
                pst = ps_tr0 if (ci // 4) % 2 == 0 else ps_tr1
                w = [tq[ci // 8], t_idb]
                if ci == 0 and tt - 1 in tx2_hist:
                    w.append(tx2_hist[tt - 1])
                if ci == 0 and tt == 1:
                    w.append(mm_g_by_io[6])
                if ci == 0 and tt >= 2:
                    w.append(wd_last[1])
                if len(tx2) >= 2 and ci % 4 == 0:
                    w.append(tx2[-2])
                ttr = step("tensor", lambda e, ci=ci, pst=pst, q2=q2: e.transpose(
                    pst[:, (ci % 4) * 128:(ci % 4 + 1) * 128],
                    q2[:, ci * 128:(ci + 1) * 128], ident_b[:]), waits=w)
                if ci % 4 == 3:
                    tx2.append(step("vector",
                                    lambda e, ci=ci, xt2=xt2, pst=pst: e.tensor_copy(
                        xt2[:, (ci - 3) * 128:(ci + 1) * 128], pst[:, 0:512]),
                        waits=[ttr]))
            ttr_hist[tt] = ttr
            tx2_hist[tt] = tx2[-1]
            t_x2s[tt] = tx2[-1]

        quant2_part(0)
        quant2_part(1)
        transpose_part(0)
        transpose_part(1)
        quant2_part(2)
        quant2_part(3)
        if tail_cb is not None:
            tail_cb(tq_hist[3][-1])
        for pair in range(2):
            if pair == 1:
                transpose_part(2)
                transpose_part(3)
            for dh in range(2):
                mm_by_tt = {}
                for g4 in range(4):
                    par = wds % 2
                    wds += 1
                    wbd = wd_buf[:, par * 4096:(par + 1) * 4096]
                    w = [t_qd, t_sx]
                    if wd_last[par] is not None:
                        w.append(wd_last[par])
                    elif par == 0:
                        w.append(mm_g_by_io[7])
                    else:
                        w.append(mm_u_by_io[7])
                    t_wda = dstep("sync", lambda e, dh=dh, g4=g4, wbd=wbd: e.dma_start(
                        wbd[:, 0:2048].rearrange("p (ci n) -> p ci n", ci=4),
                        wdq_d.ap().rearrange("(ci p) n -> p ci n", p=128)
                        [:, g4 * 8:g4 * 8 + 4, dh * 512:(dh + 1) * 512]),
                        "d_wd0" if par == 0 else "d_wd1", waits=w)
                    t_wdb = dstep("sync", lambda e, dh=dh, g4=g4, wbd=wbd: e.dma_start(
                        wbd[:, 2048:4096].rearrange("p (ci n) -> p ci n", ci=4),
                        wdq_d.ap().rearrange("(ci p) n -> p ci n", p=128)
                        [:, g4 * 8 + 4:g4 * 8 + 8, dh * 512:(dh + 1) * 512]),
                        "d_wgu0" if par == 0 else "d_wgu1", waits=list(w))
                    mm = None
                    for tt in (2 * pair, 2 * pair + 1):
                        xt2 = xiT2[tt % 2]
                        bank = dn_bank[(tt % 2, dh)]
                        for ci in range(8):
                            cig = g4 * 8 + ci
                            w0 = [t_wda if ci < 4 else t_wdb, t_x2s[tt]]
                            if cig == 0 and ci == 0:
                                w0.append(ev_by_bank.get((tt % 2, dh), t_gu[-1]))
                            mm = step("tensor", lambda e, ci=ci, cig=cig, xt2=xt2,
                                      bank=bank, wbd=wbd: e.matmul(
                                bank[:], xt2[:, cig * 128:(cig + 1) * 128],
                                wbd[:, ci * 512:(ci + 1) * 512],
                                start=(cig == 0), stop=(cig == IC - 1)),
                                waits=w0)
                        mm_by_tt[tt] = mm
                    wd_last[par] = mm
                for tt in (2 * pair, 2 * pair + 1):
                    bank = dn_bank[(tt % 2, dh)]
                    ev_by_bank[(tt % 2, dh)] = step(
                        "vector", lambda e, tt=tt, dh=dh, bank=bank: e.scalar_tensor_tensor(
                            r_buf[:, tt * D + dh * 512:tt * D + (dh + 1) * 512], bank[:],
                            qsc[:, 24 + tt:25 + tt],
                            r_buf[:, tt * D + dh * 512:tt * D + (dh + 1) * 512],
                            OP.mult, OP.add,
                            accum_out=colm[:, 40 + tt * 2 + dh:41 + tt * 2 + dh]),
                        waits=[mm_by_tt[tt], t_cds[tt]])
                    ev2.append(ev_by_bank[(tt % 2, dh)])
        if not stats:
            return ev2[-1], ev2[-1], tq_hist[3][-1]
        t_sx2 = step("vector", lambda e: e.tensor_add(
            sumx[:], colm[:].rearrange("p (a b) -> p a b", a=32, b=2)[:, 20:24, 0],
            colm[:].rearrange("p (a b) -> p a b", a=32, b=2)[:, 20:24, 1]), waits=[ev2[-1]])
        tsq = None
        for tt in range(TT):
            tsq = step("scalar", lambda e, tt=tt: e.activation(
                dump[:], r_buf[:, tt * D:(tt + 1) * D], AF.Square,
                accum_out=sumsq[:, tt:tt + 1]), waits=[ev2[-1]] + ([tsq] if tsq else []))
        return t_sx2, tsq

    def stage_D(k, t_sx, t_sq):
        toks = ln_stats(t_sx, t_sq, None, None, extra="attn_b")
        t_ln = toks[-1]
        t_ln2 = toks[-2]
        t_u = []
        for tt in range(TT):
            t_u.append(step("scalar", lambda e, tt=tt: e.activation(
                tokmaj[:, tt * D:(tt + 1) * D], r_buf[:, tt * D:(tt + 1) * D], AF.Identity,
                bias=sln[:, 7 * TT + tt:7 * TT + tt + 1],
                scale=sln[:, 6 * TT + tt:6 * TT + tt + 1]), waits=[t_ln, t_ln2]))
        tev = []
        for tt in range(TT):
            for c in range(DC):
                pst = ps_x[c % 2]
                w = [t_u[tt], t_id]
                if len(tev) >= 2:
                    w.append(tev[-2])
                ttr = step("tensor", lambda e, tt=tt, c=c, pst=pst: e.transpose(
                    pst[:, 0:128], tokmaj[:, tt * D + c * 128:tt * D + (c + 1) * 128],
                    ident_f[:]), waits=w)
                tev.append(step("vector", lambda e, tt=tt, c=c, pst=pst: e.tensor_scalar(
                    xnT_b[:, c * TC + tt * 128:c * TC + (tt + 1) * 128], pst[:, 0:128],
                    vp_col("gamma_b", c), vp_col("beta_b", c), OP.mult, OP.add),
                    waits=[ttr, t_vp]))
        t_xnb = tev[-1]
        t_xnbb = step("vector", lambda e: e.tensor_copy(xnTb_bf[:], xnT_b[:]), waits=[t_xnb])
        mm = None
        for c in range(DC):
            mm = step("tensor", lambda e, c=c: e.matmul(
                ps_sm[0:64, :], wbc_b[:, c * 64:(c + 1) * 64], xnTb_bf[:, c * TC:(c + 1) * TC],
                start=(c == 0), stop=(c == DC - 1)), waits=[t_xnbb, t_wbc])
        t_bc = step("vector", lambda e: e.tensor_scalar(
            bcstg[0:64, :], ps_sm[0:64, :], vp_col("bcb")[0:64, :], None, OP.add), waits=[mm, t_vp])
        t_d1 = dstep("sync", lambda e: e.dma_start(bc_d.ap(), bcstg[0:64, :]), "d_bcs", waits=[t_bc])
        t_d2 = dstep("sync", lambda e: e.dma_start(
            bcsrcB[:].rearrange("p (a n) -> p a n", a=6)[0:65:32, :, :],
            bc_d.ap()[0:18, :].rearrange("(a b) n -> b a n", b=3)), "d_bcl", waits=[t_d1])
        t_d3 = dstep("sync", lambda e: e.dma_start(
            bcsrcC[:].rearrange("p (a n) -> p a n", a=6)[0:65:32, :, :],
            bc_d.ap()[32:50, :].rearrange("(a b) n -> b a n", b=3)), "d_bcl", waits=[t_d1])
        t_d2 = t_d3
        tbc = []
        for n in range(N16):
            a, b2 = n // 3, n % 3
            srcB = bcsrcB[32 * b2:32 * b2 + 1, a * 512:(a + 1) * 512]
            srcC = bcsrcC[32 * b2:32 * b2 + 1, a * 512:(a + 1) * 512]
            w = [t_d2, t_onb]
            if len(tbc) >= 2:
                w.append(tbc[-2])
            t1 = step("tensor", lambda e, srcB=srcB, b2=b2: e.matmul(
                ps_m[0][:], ones3_b[32 * b2:32 * b2 + 1, :], srcB, start=True, stop=True), waits=w)
            tbc.append(step("vector", lambda e, n=n: e.tensor_copy(
                bcastB[:, n * TC:(n + 1) * TC], ps_m[0][:]), waits=[t1]))
            w = [t_d3, t_onb]
            if len(tbc) >= 2:
                w.append(tbc[-2])
            t2 = step("tensor", lambda e, srcC=srcC, b2=b2: e.matmul(
                ps_m[1][:], ones3_b[32 * b2:32 * b2 + 1, :], srcC, start=True, stop=True), waits=w)
            tbc.append(step("scalar", lambda e, n=n: e.activation(
                bcastC[:, n * TC:(n + 1) * TC], ps_m[1][:], AF.Identity), waits=[t2]))
        t_bcast = tbc[-1]
        t_bcastB = tbc[-2]
        t_r3p = []
        for tt in range(TT):
            t_r3p.append(step("scalar", lambda e, tt=tt: e.activation(
                tokmaj[:, tt * D:(tt + 1) * D], r_buf[:, tt * D:(tt + 1) * D], AF.Identity,
                bias=sln[:, 9 * TT + tt:9 * TT + tt + 1],
                scale=sln[:, 8 * TT + tt:8 * TT + tt + 1]), waits=[t_xnb, t_ln, t_ln2]))
        prev_y = None
        prev_scan = None
        prev_add = None
        prev_tch = None
        padds = {}
        t1_prev = None
        mm_prev = None
        for c in range(DC):
            w = [t_wdel, t_sx, t_bcast, t_bcastB]
            if mm_prev is not None:
                w.append(mm_prev)
            t_wd2 = dstep("sync", lambda e, c=c: e.dma_start(
                wdelbuf[:].rearrange("p (ci n) -> p ci n", ci=DC),
                w_delta_d.ap().rearrange("(ci p) n -> p ci n", p=128)[:, :, c * 128:(c + 1) * 128]),
                "d_wdel", waits=w)
            mm = None
            for ci in range(DC):
                w0 = [t_wd2, t_xnbb]
                if t1_prev is not None and ci == 0:
                    w0.append(t1_prev)
                mm = step("tensor", lambda e, ci=ci: e.matmul(
                    ps_m[0][:], wdelbuf[:, ci * 128:(ci + 1) * 128],
                    xnTb_bf[:, ci * TC:(ci + 1) * TC], start=(ci == 0), stop=(ci == DC - 1)),
                    waits=w0)
            mm_prev = mm
            t1 = step("scalar", lambda e, c=c: e.activation(dump[:, 0:TC], ps_m[0][:], AF.Exp,
                                                            bias=vp_col("b_delta", c)), waits=[mm])
            t1_prev = t1
            t2 = step("scalar", lambda e: e.activation(dsm[:, 0:TC], dump[:, 0:TC], AF.Ln,
                                                       bias=np.float32(1.0)), waits=[t1])
            t3 = step("vector", lambda e, c=c: e.tensor_mul(
                dsm[:, TC:2 * TC], dsm[:, 0:TC], xnT_b[:, c * TC:(c + 1) * TC]),
                waits=[t2, t_xnb])
            for half in range(NSUB):
                h0 = half * TSUB
                w = [t2]
                if prev_tch is not None:
                    w.append(prev_tch)
                tz = step("scalar", lambda e: e.activation(
                    slabA[:].rearrange("p (n w) -> p n w", n=N16)[:, :, 0],
                    slabA[:].rearrange("p (n w) -> p n w", n=N16)[:, :, 0],
                    AF.Identity, scale=np.float32(0.0)), waits=w)
                tdA = None
                for n in range(N16):
                    tdA = step("scalar", lambda e, c=c, n=n, h0=h0: e.activation(
                        slabA[:, n * (TSUB + 1) + 1:(n + 1) * (TSUB + 1)],
                        dsm[:, h0:h0 + TSUB], AF.Exp,
                        scale=A_b[:, c * N16 + n:c * N16 + n + 1]), waits=[tz, t_ab])
                if k == 0 and c == 0 and half == 0:
                    rwait("vector", "vWB", "sAGb")
                w = [t3]
                if prev_scan is not None:
                    w.append(prev_scan)
                if prev_y is not None:
                    w.append(prev_y)
                tg2 = step("vector", lambda e, c=c: e.tensor_copy(
                    slabB[:].rearrange("p (n w) -> p n w", n=N16)[:, :, 0],
                    carryB[:, c * N16:(c + 1) * N16]),
                    waits=w + ([t_init] if k == 0 and c == 0 and half == 0 else []))
                tb = None
                for n in range(N16):
                    tb = step("vector", lambda e, c=c, n=n, h0=h0: e.tensor_mul(
                        slabB[:, n * (TSUB + 1) + 1:(n + 1) * (TSUB + 1)],
                        dsm[:, TC + h0:TC + h0 + TSUB],
                        bcastB[:, n * TC + h0:n * TC + h0 + TSUB]), waits=[tg2, t_bcast])
                tscan = step("vector", lambda e: e.tensor_tensor_scan(
                    slabA[:], slabA[:], slabB[:], 0.0, OP.mult, OP.add), waits=[tdA, tb])
                prev_scan = tscan
                tc2 = step("vector", lambda e, c=c: e.tensor_copy(
                    carryB[:, c * N16:(c + 1) * N16],
                    slabA[:].rearrange("p (n w) -> p n w", n=N16)[:, :, TSUB]), waits=[tscan])
                if k == min(NCH, DBG_CHUNKS) - 1 and c == DC - 1 and half == NSUB - 1:
                    emit("vector", lambda e: e.engine_nop(), waits=[tc2], inc="sCb", amt=1)
                tch = None
                for n in range(N16):
                    tch = step("vector", lambda e, c=c, n=n, h0=h0: e.tensor_mul(
                        slabB[:, n * TSUB:(n + 1) * TSUB],
                        slabA[:, n * (TSUB + 1) + 1:(n + 1) * (TSUB + 1)],
                        bcastC[:, n * TC + h0:n * TC + h0 + TSUB]),
                        waits=[tc2, t_bcast] if n == 0 else [])
                prev_tch = tch
                tw = tch
                for wdt in (8, 4, 2):
                    tw = step("vector", lambda e, wdt=wdt: e.tensor_add(
                        slabB[:, 0:wdt * TSUB], slabB[:, 0:wdt * TSUB],
                        slabB[:, wdt * TSUB:2 * wdt * TSUB]), waits=[tw])
                prev_y = step("vector", lambda e, h0=h0: e.tensor_add(
                    dsm[:, 2 * TC + h0:2 * TC + h0 + TSUB],
                    slabB[:, 0:TSUB], slabB[:, TSUB:2 * TSUB]), waits=[tw])
            t_yd = step("vector", lambda e, c=c: e.scalar_tensor_tensor(
                dsm[:, 2 * TC:3 * TC], xnT_b[:, c * TC:(c + 1) * TC], vp_col("Dsk_b", c),
                dsm[:, 2 * TC:3 * TC], OP.mult, OP.add), waits=[prev_y])
            for tt in range(TT):
                bank = ps_h if tt % 2 == 0 else ps_sm
                w = [t_yd, t_id, t_r3p[tt]]
                if padds.get(tt % 2) is not None:
                    w.append(padds[tt % 2])
                ttr = step("tensor", lambda e, c=c, tt=tt, bank=bank: e.transpose(
                    bank[:, 0:128], dsm[:, 2 * TC + tt * 128:2 * TC + (tt + 1) * 128],
                    ident_f[:]), waits=w)
                prev_add = step("vector", lambda e, c=c, tt=tt, bank=bank: e.tensor_add(
                    tokmaj[:, tt * D + c * 128:tt * D + (c + 1) * 128],
                    tokmaj[:, tt * D + c * 128:tt * D + (c + 1) * 128],
                    bank[:, 0:128]), waits=[ttr])
                padds[tt % 2] = prev_add
            prev_y = prev_add
        t_mv = None
        for tt in range(TT):
            for dh in range(2):
                t_mv = step("vector", lambda e, tt=tt, dh=dh: e.scalar_tensor_tensor(
                    r_buf[:, tt * D + dh * 512:tt * D + (dh + 1) * 512],
                    tokmaj[:, tt * D + dh * 512:tt * D + (dh + 1) * 512], 1.0,
                    tokmaj[:, tt * D + dh * 512:tt * D + (dh + 1) * 512], OP.mult, OP.bypass,
                    accum_out=colm[:, 40 + tt * 2 + dh:41 + tt * 2 + dh]), waits=[prev_add])
        t_sx2 = step("vector", lambda e: e.tensor_add(
            sumx[:], colm[:].rearrange("p (a b) -> p a b", a=32, b=2)[:, 20:24, 0],
            colm[:].rearrange("p (a b) -> p a b", a=32, b=2)[:, 20:24, 1]), waits=[t_mv])
        tsq = None
        for tt in range(TT):
            tsq = step("scalar", lambda e, tt=tt: e.activation(
                dump[:], r_buf[:, tt * D:(tt + 1) * D], AF.Square,
                accum_out=sumsq[:, tt:tt + 1]), waits=[t_mv] + ([tsq] if tsq else []))
        return t_sx2, tsq

    if DBG_CUT >= 4 and not (30 <= DBG_CUT < 40):
        ag_block("sCa", "gKA", ag_a_in, ag_a_out, agoA, carryA, DC, "sAGa", t_init)
    if stages >= 3 and DBG_CUT >= 6:
        ag_block("sCb", "gKB", ag_b_in, ag_b_out, agoB, carryB, 128, "sAGb", t_init)

    if DBG_CUT < 2:
        t_final = dstep("sync", lambda e: e.dma_start(
            out_sh[0:TC, :].rearrange("(tt p) d -> p tt d", p=128),
            r_buf[:].rearrange("p (tt d) -> p tt d", tt=TT)), "d_out", waits=[t_prologue])
        emit("sync", lambda e: e.nop(), waits=[t_final], inc="sync_c", amt=1)
        return _finish(nc, q)
    th0 = halo_token(0, waits=[t_prologue])
    hv = None
    hv_prev = None
    for dp in range(4):
        wb = wgubuf[dp % 2]
        w = [t_win, th0]
        if hv_prev is not None:
            w.append(hv_prev)
        t_ws = dstep("sync", lambda e, dp=dp, wb=wb: e.dma_start(
            wb[:, 0:2048].rearrange("p (c n) -> p c n", c=DC),
            w_in_d.ap().rearrange("(c p) n -> p c n", p=128)[:, :, dp * 256:(dp + 1) * 256]),
            "d_wgu0" if dp % 2 == 0 else "d_wgu1", waits=w)
        for j in range(2):
            for c in range(DC):
                hv = step("tensor", lambda e, dp=dp, c=c, j=j, wb=wb: e.matmul(
                    ps_h[:, 2 * dp + j:2 * dp + j + 1],
                    wb[:, c * 256 + j * 128:c * 256 + (j + 1) * 128],
                    xnTh[:, c:c + 1], start=(c == 0), stop=(c == DC - 1)),
                    waits=[t_ws] if c == 0 else [])
        hv_prev = hv
    t_h0 = step("vector", lambda e: e.tensor_scalar(
        xbh_prev[:], ps_h[:, 0:DC], vp_col("hm", 0), None, OP.mult), waits=[hv, t_vp, t_init])

    t_entry = [t_h0]
    a_pre = None
    t_outprev = None
    for k in range(min(NCH, DBG_CHUNKS)):
        if DBG_CUT < 3:
            break
        tA, tHev, tLh, tSv = stage_A(k, t_entry, a_pre)
        tsx, tsq = (tA, tHev)
        t_r1l = tA
        if DBG_CUT >= 4 and not (30 <= DBG_CUT < 40):
            tsx, tsq, t_r1l = stage_B(k, tA, tHev, tLh, t_outprev)
        if stages >= 2 and DBG_CUT >= 5 and not (30 <= DBG_CUT < 40):
            tsx, tsq = stage_FFN(k, tsx, tsq, 4, 5)
        if stages >= 3 and DBG_CUT >= 6 and not (30 <= DBG_CUT < 40):
            tsx, tsq = stage_D(k, tsx, tsq)
        if stages >= 4 and DBG_CUT >= 7 and not (30 <= DBG_CUT < 40):
            a_pre_next = {}
            cb = None
            if k + 1 < min(NCH, DBG_CHUNKS):
                cb = (lambda t_aln, k=k, t_r1l=t_r1l:
                      a_pre_next.update(stage_A_pre(k + 1, [t_r1l, t_aln])))
            tsx, tsq, t_aln = stage_FFN(k, tsx, tsq, 10, 11, stats=False, tail_cb=cb)
            a_pre = a_pre_next if a_pre_next else None
        t_out = dstep("sync", lambda e, k=k: e.dma_start(
            out_sh[k * TC:(k + 1) * TC, :].rearrange("(tt p) d -> p tt d", p=128),
            r_buf[:].rearrange("p (tt d) -> p tt d", tt=TT)), "d_out", waits=[tsx, tsq, tSv])
        t_outprev = t_out
        t_entry = [t_r1l, tsx, tsq]
    t_done = t_outprev if t_outprev is not None else t_h0

    emit("sync", lambda e: e.nop(), waits=[t_done], inc="sync_c", amt=1)
    return _finish(nc, q)


def _finish(nc, q):
    with nc.Block() as block:
        @block.sync
        def _(eng):
            for f in q["sync"]:
                f(eng)

        @block.scalar
        def _(eng):
            for f in q["scalar"]:
                f(eng)

        @block.vector
        def _(eng):
            for f in q["vector"]:
                f(eng)

        @block.tensor
        def _(eng):
            for f in q["tensor"]:
                f(eng)

        @block.gpsimd
        def _(eng):
            for f in q["gpsimd"]:
                f(eng)

    return nc


def _prep_inputs(inputs):
    f = np.float32
    B, L = 4, 8192
    x = np.ascontiguousarray(np.asarray(inputs["L0_attn_a_x"], f))

    vpar = np.zeros((128, NV), f)

    def put(name, arr):
        c0, w = VP.cols[name]
        vpar[:, c0:c0 + w] = arr

    def fm(v):
        return np.ascontiguousarray(np.asarray(v, f).reshape(DC, 128).T)

    put("gamma_a", fm(inputs["L0_attn_a_gamma"]))
    put("beta_a", fm(inputs["L0_attn_a_beta"]))
    cw = np.asarray(inputs["L0_attn_a_conv_w"], f).reshape(3, D)
    cwf = np.zeros((128, 3 * DC), f)
    for c in range(DC):
        for kk in range(3):
            cwf[:, c * 3 + kk] = cw[kk, c * 128:(c + 1) * 128]
    put("conv_w", cwf)
    put("conv_b", fm(inputs["L0_attn_a_conv_b"]))
    put("b_dt", fm(inputs["L0_attn_a_b_dt"]))
    put("A_a", fm(inputs["L0_attn_a_A"]))
    put("Dsk_a", fm(inputs["L0_attn_a_D_skip"]))
    put("gamma_b", fm(inputs["L0_attn_b_gamma"]))
    put("beta_b", fm(inputs["L0_attn_b_beta"]))
    put("b_delta", fm(inputs["L0_attn_b_b_delta"]))
    put("Dsk_b", fm(inputs["L0_attn_b_D_skip"]))
    al = np.asarray(inputs["L0_attn_b_A_log"], f)
    alf = np.zeros((128, DC * N16), f)
    for c in range(DC):
        alf[:, c * N16:(c + 1) * N16] = al[c * 128:(c + 1) * 128, :]
    put("A_logb", alf)
    bcb = np.zeros((128, 1), f)
    bcb[0:16, 0] = np.asarray(inputs["L0_attn_b_b_B"], f)
    bcb[32:48, 0] = np.asarray(inputs["L0_attn_b_b_C"], f)
    put("bcb", bcb)
    put("onescol", np.ones((128, 1), f))

    lr = np.zeros((D, 128), f)
    lr[:, 0:64] = np.asarray(inputs["L0_attn_a_W_xp"], f)
    lr[:, 64:80] = np.asarray(inputs["L0_attn_a_W_Bg"], f)
    lr[:, 96:112] = np.asarray(inputs["L0_attn_a_W_Cg"], f)
    wbc = np.zeros((D, 64), f)
    wbc[:, 0:16] = np.asarray(inputs["L0_attn_b_W_B"], f)
    wbc[:, 32:48] = np.asarray(inputs["L0_attn_b_W_C"], f)

    shared = {
        "w_in": np.ascontiguousarray(np.asarray(inputs["L0_attn_a_W_in"], f)),
        "lr_w": np.ascontiguousarray(lr),
        "w_dt": np.ascontiguousarray(np.asarray(inputs["L0_attn_a_W_dt"], f)),
        "w_bp": np.ascontiguousarray(np.asarray(inputs["L0_attn_a_W_Bp"], f)),
        "w_cp": np.ascontiguousarray(np.asarray(inputs["L0_attn_a_W_Cp"], f)),
        "w_out": np.ascontiguousarray(np.asarray(inputs["L0_attn_a_W_out"], f)),
        "w_delta": np.ascontiguousarray(np.asarray(inputs["L0_attn_b_W_delta"], f)),
        "w_bc": np.ascontiguousarray(wbc),
        "wg_T": np.ascontiguousarray(np.asarray(inputs["L0_ffn_w_gate"], f).T),
        "wu_T": np.ascontiguousarray(np.asarray(inputs["L0_ffn_w_up"], f).T),
        "wd_T": np.ascontiguousarray(np.asarray(inputs["L0_ffn_w_down"], f).T),
        "ident_in": np.eye(128, dtype=f),
        "ones_row": np.ones((1, 128), f),
    }

    in_maps = []
    for core in range(8):
        b, h = core // 2, core % 2
        lo = h * TLOC
        xs = np.zeros((TLOC + 2, D), f)
        xs[1:TLOC + 1] = x[b, lo:lo + TLOC]
        if lo > 0:
            xs[0] = x[b, lo - 1]
        if lo + TLOC < L:
            xs[TLOC + 1] = x[b, lo + TLOC]
        vpc = vpar.copy()
        c0, _ = VP.cols["hm"]
        vpc[:, c0] = 0.0 if h == 0 else 1.0
        vpc[:, c0 + 1] = 0.0 if h == 1 else 1.0
        c0, _ = VP.cols["msk"]
        vpc[:, c0] = 1.0 if h == 1 else 0.0
        odd = (h == 1)
        cfgv = np.zeros((1, 8), np.int32)
        cfgv[0, 0] = 1 if odd else 0
        cfgv[0, 1] = 1 if odd else 0
        cfgv[0, 2] = 0 if odd else 1
        cfgv[0, 3] = 0 if odd else 1
        m = {"x_sh": xs, "vparams": np.ascontiguousarray(vpc), "cfg": cfgv}
        m.update(shared)
        in_maps.append(m)
    return in_maps


def kernel(**inputs):
    in_maps = _prep_inputs(inputs)
    nc = build_nc(STAGES)
    res = run_bass_kernel_spmd(nc, in_maps, core_ids=list(range(8)))
    out = np.empty((4, 8192, D), np.float32)
    for core in range(8):
        b, h = core // 2, core % 2
        out[b, h * TLOC:(h + 1) * TLOC] = res.results[core]["out_sh"]
    return out


# version 13 (final)
# speedup vs baseline: 1.0101x; 1.0101x over previous
import numpy as np
import concourse.bass as bass
import concourse.mybir as mybir
from concourse.bass_utils import run_bass_kernel_spmd

F32 = mybir.dt.float32
BF16 = mybir.dt.bfloat16
I8 = mybir.dt.int8
I32 = mybir.dt.int32
AF = mybir.ActivationFunctionType
OP = mybir.AluOpType
AX = mybir.AxisListType.X

D = 1024
DC = 8
TC = 512
TT = 4
NCH = 8
TLOC = NCH * TC
N16 = 16
INNER = 4096
IC = 32
TSUB = 256
NSUB = TC // TSUB
SLAB = N16 * (TSUB + 1)
MAGIC = float(np.float32(12582912.0))
EPS = 1e-5
STAGES = 4
DBG_CHUNKS = NCH
DBG_CUT = 99
DBG_DUMP = False


class VP:
    cols = {}
    nv = 0

    @classmethod
    def add(cls, name, w):
        cls.cols[name] = (cls.nv, w)
        cls.nv += w


VP.add("gamma_a", DC); VP.add("beta_a", DC)
VP.add("conv_w", 3 * DC)
VP.add("conv_b", DC)
VP.add("b_dt", DC)
VP.add("A_a", DC)
VP.add("Dsk_a", DC)
VP.add("gamma_b", DC); VP.add("beta_b", DC)
VP.add("b_delta", DC)
VP.add("Dsk_b", DC)
VP.add("A_logb", DC * N16)
VP.add("bcb", 1)
VP.add("hm", 2)
VP.add("msk", 1)
VP.add("onescol", 1)
NV = VP.nv


def build_nc(stages=STAGES, dbg_chunks=None, dbg_cut=None):
    global DBG_CHUNKS, DBG_CUT
    if dbg_chunks is not None:
        DBG_CHUNKS = dbg_chunks
    if dbg_cut is not None:
        DBG_CUT = dbg_cut
    nc = bass.Bass("TRN2", target_bir_lowering=False)

    x_sh = nc.dram_tensor("x_sh", [TLOC + 2, D], F32, kind="ExternalInput").ap()
    out_sh = nc.dram_tensor("out_sh", [TLOC, D], F32, kind="ExternalOutput").ap()
    dbg_xi = nc.dram_tensor("dbg_xi", [128, 4096], F32, kind="ExternalOutput").ap()
    dbg_gu = nc.dram_tensor("dbg_gu", [128, 16384], F32, kind="ExternalOutput").ap()
    dbg_r2 = nc.dram_tensor("dbg_r2", [128, 4096], F32, kind="ExternalOutput").ap()
    w_in = nc.dram_tensor("w_in", [D, 2 * D], F32, kind="ExternalInput").ap()
    lr_w = nc.dram_tensor("lr_w", [D, 128], F32, kind="ExternalInput").ap()
    w_dt = nc.dram_tensor("w_dt", [64, D], F32, kind="ExternalInput").ap()
    w_bp = nc.dram_tensor("w_bp", [16, D], F32, kind="ExternalInput").ap()
    w_cp = nc.dram_tensor("w_cp", [16, D], F32, kind="ExternalInput").ap()
    w_out = nc.dram_tensor("w_out", [D, D], F32, kind="ExternalInput").ap()
    w_delta = nc.dram_tensor("w_delta", [D, D], F32, kind="ExternalInput").ap()
    w_bc = nc.dram_tensor("w_bc", [D, 64], F32, kind="ExternalInput").ap()
    wg_T = nc.dram_tensor("wg_T", [D, INNER], F32, kind="ExternalInput").ap()
    wu_T = nc.dram_tensor("wu_T", [D, INNER], F32, kind="ExternalInput").ap()
    wd_T = nc.dram_tensor("wd_T", [INNER, D], F32, kind="ExternalInput").ap()
    vparams = nc.dram_tensor("vparams", [128, NV], F32, kind="ExternalInput").ap()
    cfg = nc.dram_tensor("cfg", [1, 8], I32, kind="ExternalInput").ap()
    ident_in = nc.dram_tensor("ident_in", [128, 128], F32, kind="ExternalInput").ap()
    ones_row = nc.dram_tensor("ones_row", [1, 128], F32, kind="ExternalInput").ap()

    w_in_d = nc.dram_tensor("w_in_d", [D, 2 * D], BF16)
    w_out_d = nc.dram_tensor("w_out_d", [D, D], BF16)
    w_delta_d = nc.dram_tensor("w_delta_d", [D, D], BF16)
    wgq_d = nc.dram_tensor("wgq_d", [D, INNER], BF16)
    wuq_d = nc.dram_tensor("wuq_d", [D, INNER], BF16)
    wdq_d = nc.dram_tensor("wdq_d", [INNER, D], BF16)
    bc_d = nc.dram_tensor("bc_d", [64, TC], BF16)
    ag_a_in = nc.dram_tensor("ag_a_in", [128, DC], F32)
    ag_a_out = nc.dram_tensor("ag_a_out", [2, 128, DC], F32)
    ag_b_in = nc.dram_tensor("ag_b_in", [128, 128], F32)
    ag_b_out = nc.dram_tensor("ag_b_out", [2, 128, 128], F32)

    S = nc.alloc_sbuf_tensor
    vp = S("vp", [128, NV], F32)
    xw = S("xw", [128, TT * D], F32)
    xb_buf = S("xb_buf", [128, DC * (TC + 2)], F32)
    sz = S("sz", [128, DC * TC], F32)
    r_buf = S("r_buf", [128, TT * D], F32)
    carryA = S("carryA", [128, DC], F32)
    carryB = S("carryB", [128, 128], F32)
    xbh_prev = S("xbh_prev", [128, DC], F32)
    ident_b = S("ident_b", [128, 128], BF16)
    ident_f = S("ident_f", [128, 128], F32)
    ones_f = S("ones_f", [1, 128], F32)
    ones_b = S("ones_b", [1, 128], BF16)
    ones3_b = S("ones3_b", [65, 128], BF16)
    cfg_t = S("cfg_t", [1, 8], I32)
    A_b = S("A_b", [128, DC * N16], F32)
    agoA = S("agoA", [128, 2 * DC], F32)
    agoB = S("agoB", [128, 2 * 128], F32)
    lr_b = S("lr_b", [128, DC * 128], BF16)
    wpc_b = S("wpc_b", [80, D], BF16)
    wcp_b2 = S("wcp_b2", [16, D], BF16)
    da1_cg = S("da1_cg", [16, TC], BF16)
    wbc_b = S("wbc_b", [128, DC * 64], BF16)
    da1_b = S("da1_b", [128, TC], BF16)
    colm = S("colm", [128, 64], F32)
    sumx = S("sumx", [128, TT], F32)
    sumsq = S("sumsq", [128, TT], F32)
    sln = S("sln", [128, 12 * TT], F32)
    hst = S("hst", [1, 16], F32)
    xnTh = S("xnTh", [128, DC], BF16)
    mcl = S("mcl", [128, 4], F32)
    qsc = S("qsc", [128, 32], F32)

    PF = 17952
    PB = 26624
    pf = S("pf", [128, PF], F32)
    pb = S("pb", [128, PB], BF16)
    pi = S("pi", [128, 1024], I8)
    dump = S("dump", [128, 1024], F32)

    x_act = pf[:, 0:4096]
    bsm = pf[:, 4096:12288]
    gu = pf[:, 0:16384]
    xn3 = pf[:, 16384:17408]
    gatebuf = pf[:, 17408:17920]
    slabA = pf[:, 0:SLAB]
    slabB = pf[:, SLAB:2 * SLAB]
    xnT_b = pf[:, 2 * SLAB:2 * SLAB + 4096]
    dsm = pf[:, 2 * SLAB + 4096:2 * SLAB + 4096 + 1536]
    tokmaj = pf[:, 13856:17952]
    pstg32 = pf[:, 0:4096]
    xhrow = pf[0:1, 0:1024]

    u_bf = pb[:, 0:4096]
    xnT_a = pb[:, 4096:8192]
    winbuf = [pb[:, 8192:10240], pb[:, 10240:12288]]
    xact_b = pb[:, 0:4096]
    g_bf = pb[:, 4096:8192]
    woutbuf = pb[:, 8192:12288]
    xi_bf = pb[:, 0:4096]
    xiT = pb[:, 4096:8192]
    wgubuf = [pb[:, 8192:12288], pb[:, 12288:16384],
              pb[:, 16384:20480], pb[:, 20480:24576]]
    q2stg = pb[:, 0:4096]
    q2stg2 = pb[:, 12288:16384]
    xiT2 = [pb[:, 4096:8192], pb[:, 8192:12288]]
    wd_buf = pb[:, 16384:24576]
    xnTb_bf = pb[:, 0:4096]
    bcstg = pb[:, 4096:4608]
    wdelbuf = pb[:, 4096:5120]
    bcsrcB = pb[:, 4096:7168]
    bcsrcC = pb[:, 7168:10240]
    bcastB = pb[:, 10240:18432]
    bcastC = pb[:, 18432:26624]
    pstg16 = pb[:, 0:4096]
    uh_b = pb[0:1, 12288:13312]

    ps_tr0 = nc.alloc_psum_tensor("ps_tr0", [128, 1024], BF16)
    ps_tr1 = nc.alloc_psum_tensor("ps_tr1", [128, 1024], BF16)
    ps_m = [nc.alloc_psum_tensor(f"ps_m{i}", [128, 512], F32) for i in range(2)]
    ps_x = [nc.alloc_psum_tensor(f"ps_x{i}", [128, 512], F32) for i in range(2)]
    ps_sm = nc.alloc_psum_tensor("ps_sm", [128, 512], F32)
    ps_h = nc.alloc_psum_tensor("ps_h", [128, 512], F32)

    assert nc.sbuf_bytes_remaining > 2048, nc.sbuf_bytes_remaining

    ENGS = ["sync", "scalar", "vector", "tensor", "gpsimd"]
    q = {e: [] for e in ENGS}
    sems = {}

    def mksem(name):
        sems[name] = [nc.alloc_semaphore(name), 0]
    for e in ENGS:
        mksem(e + "_c")
        mksem(e + "_d")
    for s in ["sCa", "sCb", "sAGa", "sAGb"]:
        mksem(s)
    regs = {}

    waited = {e: {} for e in ENGS}

    def emit(eng, fn, waits=(), inc=None, amt=1):
        wd = waited[eng]
        W = []
        for (s, v) in waits:
            if s is None:
                continue
            if wd.get(s, -1) >= v:
                continue
            wd[s] = v
            W.append((sems[s][0], v))
        H = None
        if inc:
            sems[inc][1] += amt
            H = sems[inc][0]

        def run(e, fn=fn, W=W, H=H, amt=amt):
            for h, v in W:
                e.wait_ge(h, v)
            r = fn(e)
            if H is not None:
                r.then_inc(H, amt)
        q[eng].append(run)

    def step(eng, fn, waits=()):
        s = eng + "_c"
        emit(eng, fn, waits=waits, inc=s, amt=1)
        return (s, sems[s][1])

    DSEMS = ["d_boot", "d_gboot", "d_stg", "d_gstg", "d_scr", "d_xw", "d_xh",
             "d_win0", "d_win1", "d_wout", "d_wgu0", "d_wgu1", "d_wd0", "d_wd1", "d_wdel",
             "d_bcs", "d_bcl", "d_ag", "d_out"]
    for s in DSEMS:
        mksem(s)

    def dstep(eng, fn, sem, waits=()):
        self_wait = (sem, sems[sem][1])
        emit(eng, fn, waits=list(waits) + [self_wait], inc=sem, amt=16)
        return (sem, sems[sem][1])

    def rwait(eng, regname, semname):
        def run(e, regname=regname, h=sems[semname][0]):
            e.wait_ge(h, regs[regname])
        q[eng].append(run)

    def vp_col(name, j=0):
        c0, w = VP.cols[name]
        return vp[:, c0 + j:c0 + j + 1]

    def vp_cols(name, j0, cnt):
        c0, w = VP.cols[name]
        return vp[:, c0 + j0:c0 + j0 + cnt]

    dstep("sync", lambda e: e.dma_start(vp[:], vparams), "d_boot")
    dstep("sync", lambda e: e.dma_start(cfg_t[:], cfg), "d_boot")
    dstep("sync", lambda e: e.dma_start(ident_f[:], ident_in), "d_boot")
    t_boot = dstep("sync", lambda e: e.dma_start(ones_f[:], ones_row), "d_boot")
    t_vp = t_cfg = t_id = t_on = t_boot

    t_idb = step("vector", lambda e: e.tensor_copy(ident_b[:], ident_f[:]), waits=[t_id])
    t_onb = step("vector", lambda e: e.tensor_copy(ones_b[:], ones_f[:]), waits=[t_on])
    for _rb in (0, 32, 64):
        t_gboot = dstep("gpsimd", lambda e, _rb=_rb: e.dma_start(ones3_b[_rb:_rb + 1, :], ones_row),
                        "d_gboot")
    t_lr = t_wpc = t_wbc = t_onb2 = t_gboot
    t_init = step("vector", lambda e: [e.memset(carryA[:], 0.0),
                                       e.memset(carryB[:], 0.0)][-1])
    t_ab1 = step("scalar", lambda e: e.activation(A_b[:], vp_cols("A_logb", 0, DC * N16), AF.Exp),
                 waits=[t_vp])
    t_ab = step("vector", lambda e: e.tensor_scalar_mul(A_b[:], A_b[:], -1.0), waits=[t_ab1])

    def reg_v(e):
        rWA = e.alloc_register("rWA"); e.load(rWA, cfg_t[0:1, 0:1])
        rWB = e.alloc_register("rWB"); e.load(rWB, cfg_t[0:1, 1:2])
        regs["vWA"], regs["vWB"] = rWA, rWB
    emit("vector", lambda e: (reg_v(e), e.engine_nop())[1], waits=[t_cfg])

    def reg_g(e):
        rKA = e.alloc_register("rKA"); e.load(rKA, cfg_t[0:1, 2:3])
        rKB = e.alloc_register("rKB"); e.load(rKB, cfg_t[0:1, 3:4])
        regs["gKA"], regs["gKB"] = rKA, rKB
    emit("gpsimd", lambda e: (reg_g(e), e.engine_nop())[1], waits=[t_cfg])

    dstep("gpsimd", lambda e: e.dma_start(lr_b[:], lr_w.rearrange("(c p) j -> p c j", p=128)), "d_gboot")
    dstep("gpsimd", lambda e: e.dma_start(wpc_b[0:64, :], w_dt), "d_gboot")
    dstep("gpsimd", lambda e: e.dma_start(wpc_b[64:80, :], w_bp), "d_gboot")
    dstep("gpsimd", lambda e: e.dma_start(wcp_b2[0:16, :], w_cp), "d_gboot")
    t_gboot = dstep("gpsimd", lambda e: e.dma_start(wbc_b[:], w_bc.rearrange("(c p) j -> p c j", p=128)), "d_gboot")
    t_lr = t_wpc = t_wbc = None

    def cast_through(src, dst, rows, cols, start_tok):
        prev = start_tok
        for i in range(rows // 128):
            tin = dstep("gpsimd", lambda e, i=i: e.dma_start(
                pstg16[:, 0:cols], src[i * 128:(i + 1) * 128, :]), "d_gstg", waits=[prev])
            prev = dstep("sync", lambda e, i=i: e.dma_start(
                dst[i * 128:(i + 1) * 128, :], pstg16[:, 0:cols]), "d_scr", waits=[tin])
        return prev

    t_win = cast_through(w_in, w_in_d.ap(), D, 2 * D, t_on)
    t_wout = cast_through(w_out, w_out_d.ap(), D, D, t_win)
    t_wdel = cast_through(w_delta, w_delta_d.ap(), D, D, t_wout)

    def quant_matrix(src, dst, rows, cols, mslot, start_tok):
        nt = rows // 128
        tacc = step("vector", lambda e: e.memset(colm[:, 0:nt], 0.0), waits=[start_tok])
        tprev = tacc
        tred = None
        for i in range(nt):
            tin = dstep("sync", lambda e, i=i: e.dma_start(
                pstg32[:, 0:cols], src[i * 128:(i + 1) * 128, :]), "d_stg",
                waits=[tprev, (tred[0], tred[1]) if tred else start_tok])
            tred = step("vector", lambda e, i=i: e.reduce_sum(
                colm[:, i:i + 1], pstg32[:, 0:cols], AX, apply_absolute_value=True), waits=[tin])
            tprev = tin
        t1 = step("vector", lambda e: e.reduce_sum(colm[:, 33:34], colm[:, 0:nt], AX), waits=[tred])
        t2 = step("tensor", lambda e: e.transpose(ps_sm[0:1, 0:128], colm[:, 33:34], ident_f[:]),
                  waits=[t1, t_id])
        t3 = step("vector", lambda e: e.reduce_sum(colm[0:1, 34:35], ps_sm[0:1, 0:128], AX), waits=[t2])
        t4 = step("vector", lambda e: e.tensor_scalar(colm[0:1, 35:36], colm[0:1, 34:35],
                                                      1.0 / (rows * cols), 1e-5, OP.mult, OP.max),
                  waits=[t3])
        t5 = step("vector", lambda e: e.reciprocal(colm[0:1, 36:37], colm[0:1, 35:36]), waits=[t4])
        t6 = step("tensor", lambda e: e.matmul(ps_sm[:, 128:130], ones_f[:], colm[0:1, 35:37],
                                               start=True, stop=True), waits=[t5, t_on, t3])
        t7 = step("vector", lambda e, mslot=mslot: e.tensor_copy(mcl[:, mslot:mslot + 1],
                                                                 ps_sm[:, 128:129]), waits=[t6])
        t8 = step("vector", lambda e, mslot=mslot: e.tensor_copy(qsc[:, 8 + mslot:9 + mslot],
                                                                 ps_sm[:, 129:130]), waits=[t6])
        tprev = t8
        for i in range(nt):
            tin = dstep("sync", lambda e, i=i: e.dma_start(
                pstg32[:, 0:cols], src[i * 128:(i + 1) * 128, :]), "d_stg", waits=[tprev])
            ta = step("vector", lambda e, mslot=mslot: e.tensor_scalar(
                pstg32[:, 0:cols], pstg32[:, 0:cols], qsc[:, 8 + mslot:9 + mslot], MAGIC,
                OP.mult, OP.add), waits=[tin])
            tb = step("vector", lambda e: e.tensor_scalar(
                pstg32[:, 0:cols], pstg32[:, 0:cols], -MAGIC, 1.0, OP.add, OP.min),
                waits=[ta])
            td = step("vector", lambda e: e.tensor_scalar_max(
                pstg16[:, 0:cols], pstg32[:, 0:cols], -1.0), waits=[tb])
            tprev = dstep("sync", lambda e, i=i: e.dma_start(
                dst[i * 128:(i + 1) * 128, :], pstg16[:, 0:cols]), "d_scr", waits=[td])
        return tprev

    if stages >= 2:
        t_qg = quant_matrix(wg_T, wgq_d.ap(), D, INNER, 0, t_wdel)
        t_qu = quant_matrix(wu_T, wuq_d.ap(), D, INNER, 1, t_qg)
        t_qd = quant_matrix(wd_T, wdq_d.ap(), INNER, D, 2, t_qu)
    else:
        t_qg = t_qu = t_qd = t_wdel
    t_scr = ("d_scr", sems["d_scr"][1])
    t_win = t_wout = t_wdel = t_qg = t_qu = t_qd = t_scr
    t_lr = t_wpc = t_wbc = t_gboot
    t_prologue = t_scr

    def ag_block(contrib_sem, reg_name, agi, ago_d, ago_t, carry_t, width, out_sem, init_tok):
        rwait("gpsimd", reg_name, contrib_sem)
        t1 = dstep("gpsimd", lambda e: e.dma_start(agi.ap(), carry_t[:]), "d_ag",
                   waits=[init_tok])
        emit("gpsimd", lambda e: e.collective_compute(
            "AllGather", OP.bypass,
            replica_groups=[[0, 1], [2, 3], [4, 5], [6, 7]],
            ins=[agi.ap().opt()],
            outs=[ago_d.ap().opt()],
        ), waits=[t1], inc="gpsimd_c", amt=1)
        tcc = ("gpsimd_c", sems["gpsimd_c"][1])
        t2 = dstep("gpsimd", lambda e: e.dma_start(
            ago_t[:].rearrange("p (s w) -> p s w", s=2),
            ago_d.ap().rearrange("s p w -> p s w")), "d_ag", waits=[tcc])
        emit("gpsimd", lambda e: e.tensor_scalar(
            carry_t[:], ago_t[:, 0:width], vp_col("msk"), None, OP.mult),
            waits=[t2, t_vp], inc=out_sem, amt=1)

    IDX = {"winbuf": 1, "wgubuf": 1, "xiT2": 1}

    def nxt(name):
        IDX[name] ^= 1
        return IDX[name]

    def ln_stats(tok_w, tok_sq, slot_s, slot_b, extra=None):
        t1 = step("vector", lambda e: e.tensor_scalar_mul(colm[:, 0:TT], sumx[:], 1.0 / D),
                  waits=[tok_w])
        t2 = step("vector", lambda e: e.tensor_scalar_mul(colm[:, 4:4 + TT], sumsq[:], 1.0 / D),
                  waits=[tok_sq])
        t3 = step("vector", lambda e: e.tensor_mul(colm[:, 8:8 + TT], colm[:, 0:TT], colm[:, 0:TT]),
                  waits=[t1])
        t4 = step("vector", lambda e: e.tensor_sub(colm[:, 12:12 + TT], colm[:, 4:4 + TT],
                                                   colm[:, 8:8 + TT]), waits=[t2, t3])
        t5 = step("scalar", lambda e: e.activation(colm[:, 16:16 + TT], colm[:, 12:12 + TT],
                                                   AF.Sqrt, bias=np.float32(EPS)), waits=[t4])
        t6 = step("vector", lambda e: e.reciprocal(colm[:, 20:20 + TT], colm[:, 16:16 + TT]),
                  waits=[t5])
        if extra is None:
            ta = step("vector", lambda e: e.tensor_copy(sln[:, slot_s * TT:(slot_s + 1) * TT],
                                                        colm[:, 20:20 + TT]), waits=[t6])
            tb = step("vector", lambda e: e.scalar_tensor_tensor(
                sln[:, slot_b * TT:(slot_b + 1) * TT], colm[:, 0:TT], -1.0,
                colm[:, 20:20 + TT], OP.mult, OP.mult), waits=[t6])
            return [ta, tb]
        t7 = step("vector", lambda e: e.tensor_mul(colm[:, 24:24 + TT], colm[:, 20:20 + TT],
                                                   colm[:, 20:20 + TT]), waits=[t6])
        t8 = step("vector", lambda e: e.tensor_mul(colm[:, 28:28 + TT], colm[:, 24:24 + TT],
                                                   colm[:, 12:12 + TT]), waits=[t7])
        t9 = step("scalar", lambda e: e.activation(colm[:, 32:32 + TT], colm[:, 28:28 + TT],
                                                   AF.Sqrt, bias=np.float32(EPS)), waits=[t8])
        t10 = step("vector", lambda e: e.reciprocal(colm[:, 36:36 + TT], colm[:, 32:32 + TT]),
                   waits=[t9])
        if extra == "attn_a":
            s_r, b_r, s_u, b_u = 0, 1, 2, 3
        else:
            s_r, b_r, s_u, b_u = 8, 9, 6, 7
        ta = step("vector", lambda e, s_r=s_r: e.tensor_scalar_add(
            sln[:, s_r * TT:(s_r + 1) * TT], colm[:, 20:20 + TT], 1.0), waits=[t6])
        tb = step("vector", lambda e, b_r=b_r: e.scalar_tensor_tensor(
            sln[:, b_r * TT:(b_r + 1) * TT], colm[:, 0:TT], -1.0, colm[:, 20:20 + TT],
            OP.mult, OP.mult), waits=[t6])
        tc = step("vector", lambda e, s_u=s_u: e.tensor_mul(
            sln[:, s_u * TT:(s_u + 1) * TT], colm[:, 20:20 + TT], colm[:, 36:36 + TT]),
            waits=[t10])
        td = step("vector", lambda e, b_u=b_u, b_r=b_r: e.tensor_mul(
            sln[:, b_u * TT:(b_u + 1) * TT], sln[:, b_r * TT:(b_r + 1) * TT],
            colm[:, 36:36 + TT]), waits=[tb, t10])
        return [ta, tb, tc, td]

    def halo_token(row_idx, waits=()):
        tin = dstep("sync", lambda e: e.dma_start(xhrow[:], x_sh[row_idx:row_idx + 1, :]),
                    "d_xh", waits=list(waits))
        t1 = step("vector", lambda e: e.reduce_sum(hst[0:1, 0:1], xhrow[:], AX), waits=[tin])
        t2 = step("scalar", lambda e: e.activation(dump[0:1, :], xhrow[:], AF.Square,
                                                   accum_out=hst[0:1, 1:2]), waits=[tin])
        t3a = step("vector", lambda e: e.tensor_scalar_mul(hst[0:1, 2:3], hst[0:1, 0:1], 1.0 / D),
                   waits=[t1])
        t3b = step("vector", lambda e: e.tensor_scalar_mul(hst[0:1, 3:4], hst[0:1, 1:2], 1.0 / D),
                   waits=[t2])
        t3c = step("vector", lambda e: e.tensor_mul(hst[0:1, 4:5], hst[0:1, 2:3], hst[0:1, 2:3]),
                   waits=[t3a])
        t3 = step("vector", lambda e: e.tensor_sub(hst[0:1, 5:6], hst[0:1, 3:4], hst[0:1, 4:5]),
                  waits=[t3b, t3c])
        t4 = step("scalar", lambda e: e.activation(hst[0:1, 6:7], hst[0:1, 5:6], AF.Sqrt,
                                                   bias=np.float32(EPS)), waits=[t3])
        t5 = step("vector", lambda e: e.reciprocal(hst[0:1, 7:8], hst[0:1, 6:7]), waits=[t4])
        t6a = step("vector", lambda e: e.tensor_mul(hst[0:1, 8:9], hst[0:1, 7:8], hst[0:1, 7:8]),
                   waits=[t5])
        t6 = step("vector", lambda e: e.tensor_mul(hst[0:1, 9:10], hst[0:1, 8:9], hst[0:1, 5:6]),
                  waits=[t6a])
        t7 = step("scalar", lambda e: e.activation(hst[0:1, 10:11], hst[0:1, 9:10], AF.Sqrt,
                                                   bias=np.float32(EPS)), waits=[t6])
        t8 = step("vector", lambda e: e.reciprocal(hst[0:1, 11:12], hst[0:1, 10:11]), waits=[t7])
        t9a = step("vector", lambda e: e.tensor_mul(hst[0:1, 12:13], hst[0:1, 7:8], hst[0:1, 11:12]),
                   waits=[t8])
        t9 = step("vector", lambda e: e.scalar_tensor_tensor(
            hst[0:1, 13:14], hst[0:1, 2:3], -1.0, hst[0:1, 12:13], OP.mult, OP.mult),
            waits=[t9a])
        t10 = step("scalar", lambda e: e.activation(uh_b[:], xhrow[:], AF.Identity,
                                                    bias=hst[0:1, 13:14], scale=hst[0:1, 12:13]),
                   waits=[t9])
        ttrs = []
        for c in range(DC):
            ttrs.append(step("tensor", lambda e, c=c: e.transpose(
                ps_tr1[:, 512 + 2 * c:512 + 2 * c + 1], uh_b[0:1, c * 128:(c + 1) * 128],
                ident_b[0:1, 0:1]), waits=[t10, t_idb]))
        evs = []
        for c in range(DC):
            evs.append(step("vector", lambda e, c=c: e.tensor_scalar(
                xnTh[:, c:c + 1], ps_tr1[:, 512 + 2 * c:512 + 2 * c + 1],
                vp_col("gamma_a", c), vp_col("beta_a", c), OP.mult, OP.add),
                waits=[ttrs[-1], t_vp]))
        return evs[-1]

    def stage_A(k, t_entry, t_early=None):
        r0 = k * TC + 1
        t_x = dstep("sync", lambda e: e.dma_start(
            xw[:].rearrange("p (tt d) -> p tt d", tt=TT),
            x_sh[r0:r0 + TC, :].rearrange("(tt p) d -> p tt d", p=128)), "d_xw",
            waits=list(t_early) if t_early is not None else list(t_entry))
        tsx = tsq = None
        for tt in range(TT):
            tsx = step("vector", lambda e, tt=tt: e.reduce_sum(
                sumx[:, tt:tt + 1], xw[:, tt * D:(tt + 1) * D], AX), waits=[t_x])
            tsq = step("scalar", lambda e, tt=tt: e.activation(
                dump[:], xw[:, tt * D:(tt + 1) * D], AF.Square,
                accum_out=sumsq[:, tt:tt + 1]),
                waits=[t_x] + ([tsq] if tsq else []))
        if DBG_CUT == 31:
            return tsx, tsq, tsq, tsq, []
        toks = ln_stats(tsx, tsq, None, None, extra="attn_a")
        t_ln = toks[-1]
        t_ln2 = toks[-2]
        if DBG_CUT == 32:
            return t_ln, t_ln2, t_ln2, t_ln2, []
        tu = []
        for tt in range(TT):
            tu.append(step("scalar", lambda e, tt=tt: e.activation(
                u_bf[:, tt * D:(tt + 1) * D], xw[:, tt * D:(tt + 1) * D], AF.Identity,
                bias=sln[:, 3 * TT + tt:3 * TT + tt + 1],
                scale=sln[:, 2 * TT + tt:2 * TT + tt + 1]),
                waits=[t_ln, t_ln2] + (list(t_entry) if tt == 0 else [])))
        if DBG_CUT == 33:
            return tu[-1], tu[-2], tu[-2], tu[-2], []
        th = halo_token((k + 1) * TC + 1, waits=[tu[-1]])
        if DBG_CUT == 34:
            return th, tu[-1], tu[-1], tu[-1], []
        tevs = []
        for tt in range(TT):
            pst = ps_tr0 if tt % 2 == 0 else ps_tr1
            ttr = []
            for c in range(DC):
                w = [tu[tt], t_idb]
                if c == 0:
                    w.append(th)
                if len(tevs) >= 9:
                    w.append(tevs[-9])
                ttr.append(step("tensor", lambda e, tt=tt, c=c, pst=pst: e.transpose(
                    pst[:, c * 128:(c + 1) * 128],
                    u_bf[:, tt * D + c * 128:tt * D + (c + 1) * 128], ident_b[:]), waits=w))
            for c in range(DC):
                tevs.append(step("vector", lambda e, tt=tt, c=c, pst=pst: e.tensor_scalar(
                    xnT_a[:, c * TC + tt * 128:c * TC + (tt + 1) * 128],
                    pst[:, c * 128:(c + 1) * 128],
                    vp_col("gamma_a", c), vp_col("beta_a", c), OP.mult, OP.add),
                    waits=[ttr[-1], t_vp]))
        t_xnT = tevs[-1]
        if DBG_CUT == 35:
            return t_xnT, tevs[-2], tevs[-2], tevs[-2], []
        ev_toks = []
        halo_done = None
        halo_toks = {}
        for dp in range(8):
            slot = nxt("winbuf")
            wb = winbuf[slot]
            w = [t_win] + list(t_entry)
            if len(ev_toks) >= 3:
                w.append(ev_toks[-3])
            if dp - 2 in halo_toks:
                w.append(halo_toks[dp - 2])
            t_ws = dstep("sync", lambda e, dp=dp, wb=wb: e.dma_start(
                wb[:].rearrange("p (c n) -> p c n", c=DC),
                w_in_d.ap().rearrange("(c p) n -> p c n", p=128)[:, :, dp * 256:(dp + 1) * 256]),
                "d_win0" if slot == 0 else "d_win1", waits=w)
            mm_t = None
            for c in range(DC):
                w0 = [t_ws, t_xnT]
                if len(ev_toks) >= 2 and c == 0:
                    w0.append(ev_toks[-2])
                step("tensor", lambda e, c=c, wb=wb: e.matmul(
                    ps_m[0][:], wb[:, c * 256:c * 256 + 128],
                    xnT_a[:, c * TC:(c + 1) * TC], start=(c == 0), stop=(c == DC - 1)), waits=w0)
                mm_t = step("tensor", lambda e, c=c, wb=wb: e.matmul(
                    ps_m[1][:], wb[:, c * 256 + 128:(c + 1) * 256],
                    xnT_a[:, c * TC:(c + 1) * TC], start=(c == 0), stop=(c == DC - 1)),
                    waits=[ev_toks[-1]] if (c == 0 and len(ev_toks) >= 1) else [])
            if dp < 4:
                for j in range(2):
                    for c in range(DC):
                        wh = [t_ws] if c == 0 else []
                        if c == 0 and j == 0:
                            wh.append(th)
                        halo_done = step("tensor", lambda e, dp=dp, c=c, j=j, wb=wb: e.matmul(
                            ps_h[:, 2 * dp + j:2 * dp + j + 1],
                            wb[:, c * 256 + j * 128:c * 256 + (j + 1) * 128],
                            xnTh[:, c:c + 1], start=(c == 0), stop=(c == DC - 1)), waits=wh)
                halo_toks[dp] = halo_done
            for j in range(2):
                do = 2 * dp + j
                if do < DC:
                    ev_toks.append(step("scalar", lambda e, do=do, j=j: e.activation(
                        xb_buf[:, do * (TC + 2) + 1:do * (TC + 2) + 1 + TC], ps_m[j][:],
                        AF.Identity), waits=[mm_t]))
                else:
                    c2 = do - DC
                    ev_toks.append(step("scalar", lambda e, c2=c2, j=j: e.activation(
                        sz[:, c2 * TC:(c2 + 1) * TC], ps_m[j][:], AF.Silu), waits=[mm_t]))
        mask_ap = vp_col("hm", 1) if k == NCH - 1 else vp_col("onescol")
        t_lh = step("vector", lambda e: e.tensor_copy(
            xb_buf[:].rearrange("p (c w) -> p c w", w=TC + 2)[:, :, 0], xbh_prev[:]),
            waits=list(t_entry) + [ev_toks[0]])
        t_hev = step("vector", lambda e, mask_ap=mask_ap: e.tensor_scalar(
            xb_buf[:].rearrange("p (c w) -> p c w", w=TC + 2)[:, :, TC + 1],
            ps_h[:, 0:DC], mask_ap, None, OP.mult), waits=[halo_done, t_vp])
        t_sv = step("vector", lambda e: e.tensor_copy(
            xbh_prev[:], xb_buf[:].rearrange("p (c w) -> p c w", w=TC + 2)[:, :, TC + 1]),
            waits=[t_hev])
        return ev_toks[-1], t_hev, t_lh, t_sv, ev_toks

    def stage_B(k, t_A, t_hev, t_lh, t_outprev, a_evs=None):
        t_r1_last = None
        W = TC + 2
        t_conv = []
        for c in range(DC):
            b = xb_buf[:, c * W:(c + 1) * W]
            cs = 3072 + (c % 2) * 512
            tae = a_evs[c] if a_evs else t_A
            t1 = step("vector", lambda e, c=c, b=b, cs=cs: e.tensor_scalar(
                bsm[:, cs:cs + TC], b[:, 0:TC], vp_col("conv_w", 3 * c + 0), None, OP.mult),
                waits=[tae, t_hev, t_lh, t_vp] + ([t_conv[-2]] if len(t_conv) >= 2 else []))
            t2 = step("vector", lambda e, c=c, b=b, cs=cs: e.scalar_tensor_tensor(
                bsm[:, cs:cs + TC], b[:, 1:1 + TC], vp_col("conv_w", 3 * c + 1),
                bsm[:, cs:cs + TC], OP.mult, OP.add), waits=[t1])
            t3 = step("vector", lambda e, c=c, b=b, cs=cs: e.scalar_tensor_tensor(
                bsm[:, cs:cs + TC], b[:, 2:2 + TC], vp_col("conv_w", 3 * c + 2),
                bsm[:, cs:cs + TC], OP.mult, OP.add), waits=[t2])
            t_conv.append(step("scalar", lambda e, c=c, cs=cs: e.activation(
                x_act[:, c * TC:(c + 1) * TC], bsm[:, cs:cs + TC], AF.Silu,
                bias=vp_col("conv_b", c)), waits=[t3]))
        t_xab = step("vector", lambda e: e.tensor_copy(xact_b[:], x_act[:]), waits=[t_conv[-1]])
        mm = None
        for c in range(DC):
            step("tensor", lambda e, c=c: e.matmul(
                ps_sm[0:80, :], lr_b[:, c * 128:c * 128 + 80], xact_b[:, c * TC:(c + 1) * TC],
                start=(c == 0), stop=(c == DC - 1)), waits=[t_xab, t_lr])
            mm = step("tensor", lambda e, c=c: e.matmul(
                ps_h[0:16, 0:TC], lr_b[:, c * 128 + 96:c * 128 + 112],
                xact_b[:, c * TC:(c + 1) * TC],
                start=(c == 0), stop=(c == DC - 1)), waits=[])
        t_da1a = step("vector", lambda e: e.tensor_copy(da1_b[0:80, :], ps_sm[0:80, :]), waits=[mm])
        t_da1 = step("vector", lambda e: e.tensor_copy(da1_cg[0:16, :], ps_h[0:16, 0:TC]),
                     waits=[t_da1a])
        gate_last = None
        pe_ev = {0: None, 1: None, 2: None}
        prev = {}
        for c in range(DC):
            o = 4096 if c % 2 else 0
            S0, S1, S2, S3, S4 = o, o + TC, o + 2 * TC, o + 3 * TC, o + 4 * TC
            w0 = [t_da1, t_wpc]
            t1 = step("tensor", lambda e, c=c: e.matmul(
                ps_m[0][:], wpc_b[0:64, c * 128:(c + 1) * 128], da1_b[0:64, :],
                start=True, stop=True), waits=w0 + ([pe_ev[0]] if pe_ev[0] else []))
            t2 = step("scalar", lambda e, c=c: e.activation(dump[:, 0:TC], ps_m[0][:], AF.Exp,
                                                            bias=vp_col("b_dt", c)),
                      waits=[t1] + ([prev.get("t3")] if prev.get("t3") else []))
            t3 = step("scalar", lambda e, c=c, S0=S0: e.activation(
                bsm[:, S0:S0 + TC], dump[:, 0:TC], AF.Ln,
                bias=np.float32(1.0)), waits=[t2])
            pe_ev[0] = t2
            prev["t3"] = t3
            t4 = step("tensor", lambda e, c=c: e.matmul(
                ps_m[1][:], wpc_b[64:80, c * 128:(c + 1) * 128], da1_b[64:80, :],
                start=True, stop=True), waits=w0 + ([pe_ev[1]] if pe_ev[1] else []))
            t5 = step("vector", lambda e, c=c, S1=S1: e.tensor_copy(
                bsm[:, S1:S1 + TC], ps_m[1][:]), waits=[t4])
            pe_ev[1] = t5
            t6 = step("tensor", lambda e, c=c: e.matmul(
                ps_x[0][:], wcp_b2[0:16, c * 128:(c + 1) * 128], da1_cg[0:16, :],
                start=True, stop=True), waits=w0 + ([pe_ev[2]] if pe_ev[2] else []))
            t7 = step("vector", lambda e, c=c, S2=S2: e.tensor_copy(
                bsm[:, S2:S2 + TC], ps_x[0][:]), waits=[t6])
            pe_ev[2] = t7
            t8 = step("vector", lambda e, c=c, S0=S0, S3=S3: e.tensor_mul(
                bsm[:, S3:S3 + TC], bsm[:, S0:S0 + TC], x_act[:, c * TC:(c + 1) * TC]),
                waits=[t3])
            t9 = step("scalar", lambda e, c=c, S0=S0: e.activation(
                bsm[:, S0:S0 + TC], bsm[:, S0:S0 + TC], AF.Exp, scale=vp_col("A_a", c)),
                waits=[t8, t_ab])
            t10 = step("vector", lambda e, c=c, S1=S1, S3=S3: e.tensor_mul(
                bsm[:, S1:S1 + TC], bsm[:, S3:S3 + TC], bsm[:, S1:S1 + TC]), waits=[t9, t5])
            if k == 0 and c == 0:
                rwait("vector", "vWA", "sAGa")
            t11 = step("vector", lambda e, c=c, S0=S0, S1=S1, S4=S4: e.tensor_tensor_scan(
                bsm[:, S4:S4 + TC], bsm[:, S0:S0 + TC], bsm[:, S1:S1 + TC],
                carryA[:, c:c + 1], OP.mult, OP.add),
                waits=[t10] + ([t_init] if k == 0 and c == 0 else [])
                + ([prev.get("t11")] if prev.get("t11") else []))
            prev["t11"] = t11
            t12 = step("vector", lambda e, c=c, S4=S4: e.tensor_copy(
                carryA[:, c:c + 1], bsm[:, S4 + TC - 1:S4 + TC]), waits=[t11])
            if k == min(NCH, DBG_CHUNKS) - 1 and c == DC - 1:
                emit("vector", lambda e: e.engine_nop(), waits=[t12], inc="sCa", amt=1)
            t13 = step("vector", lambda e, c=c, S2=S2, S4=S4: e.tensor_mul(
                bsm[:, S2:S2 + TC], bsm[:, S2:S2 + TC], bsm[:, S4:S4 + TC]),
                waits=[t11, t7])
            t14 = step("vector", lambda e, c=c, S2=S2: e.scalar_tensor_tensor(
                bsm[:, S2:S2 + TC], x_act[:, c * TC:(c + 1) * TC], vp_col("Dsk_a", c),
                bsm[:, S2:S2 + TC], OP.mult, OP.add), waits=[t13])
            t15 = step("vector", lambda e, c=c, S2=S2: e.tensor_mul(
                bsm[:, S2:S2 + TC], bsm[:, S2:S2 + TC], sz[:, c * TC:(c + 1) * TC]),
                waits=[t14] + ([a_evs[DC + c]] if a_evs else []))
            gate_last = step("vector", lambda e, c=c, S2=S2: e.tensor_copy(
                g_bf[:, c * TC:(c + 1) * TC], bsm[:, S2:S2 + TC]), waits=[t15])
        ev = []
        for dh in range(2):
            w = [t_wout, t_A]
            if len(ev) >= 2:
                w.append(ev[-1])
            t_w = dstep("sync", lambda e, dh=dh: e.dma_start(
                woutbuf[:].rearrange("p (c n) -> p c n", c=DC),
                w_out_d.ap().rearrange("(c p) n -> p c n", p=128)[:, :, dh * 512:(dh + 1) * 512]),
                "d_wout", waits=w)
            for tt in range(TT):
                mm = None
                for c in range(DC):
                    w0 = [t_w, gate_last]
                    if len(ev) >= 2 and c == 0:
                        w0.append(ev[-2])
                    mm = step("tensor", lambda e, tt=tt, c=c: e.matmul(
                        ps_m[tt % 2][:], g_bf[:, c * TC + tt * 128:c * TC + (tt + 1) * 128],
                        woutbuf[:, c * 512:(c + 1) * 512], start=(c == 0), stop=(c == DC - 1)),
                        waits=w0)
                t_r1 = step("scalar", lambda e, tt=tt, dh=dh: e.activation(
                    r_buf[:, tt * D + dh * 512:tt * D + (dh + 1) * 512],
                    xw[:, tt * D + dh * 512:tt * D + (dh + 1) * 512], AF.Identity,
                    bias=sln[:, 1 * TT + tt:1 * TT + tt + 1],
                    scale=sln[:, 0 * TT + tt:0 * TT + tt + 1]),
                    waits=[mm] + ([t_outprev] if t_outprev else []))
                t_r1_last = t_r1
                ev.append(step("vector", lambda e, tt=tt, dh=dh: e.scalar_tensor_tensor(
                    r_buf[:, tt * D + dh * 512:tt * D + (dh + 1) * 512], ps_m[tt % 2][:], 1.0,
                    r_buf[:, tt * D + dh * 512:tt * D + (dh + 1) * 512], OP.mult, OP.add,
                    accum_out=colm[:, 40 + tt * 2 + dh:41 + tt * 2 + dh]), waits=[t_r1]))
        t_sx = step("vector", lambda e: e.tensor_add(
            sumx[:], colm[:].rearrange("p (a b) -> p a b", a=32, b=2)[:, 20:24, 0],
            colm[:].rearrange("p (a b) -> p a b", a=32, b=2)[:, 20:24, 1]), waits=[ev[-1]])
        tsq = None
        for tt in range(TT):
            tsq = step("scalar", lambda e, tt=tt: e.activation(
                dump[:], r_buf[:, tt * D:(tt + 1) * D], AF.Square,
                accum_out=sumsq[:, tt:tt + 1]), waits=[ev[-1]] + ([tsq] if tsq else []))
        return t_sx, tsq, t_r1_last

    def stage_FFN(k, t_sx, t_sq, slot_s, slot_b, stats=True):
        toks = ln_stats(t_sx, t_sq, slot_s, slot_b, extra=None)
        t_ln = toks[-1]
        t_ln2 = toks[-2]
        t_xi = None
        t_coef = None
        t6_hist = {}
        txi_hist = {}
        for tt in range(TT):
            sc2 = 0 if tt % 2 == 0 else 28
            xnb = xn3 if tt % 2 == 0 else pf[:, 0:1024]
            w1 = [t_ln, t_ln2]
            if tt - 2 in t6_hist:
                w1.append(t6_hist[tt - 2])
            t1 = step("scalar", lambda e, tt=tt, xnb=xnb: e.activation(
                xnb, r_buf[:, tt * D:(tt + 1) * D], AF.Identity,
                bias=sln[:, slot_b * TT + tt:slot_b * TT + tt + 1],
                scale=sln[:, slot_s * TT + tt:slot_s * TT + tt + 1]), waits=w1)
            t2 = step("vector", lambda e, sc2=sc2, xnb=xnb: e.reduce_max(
                qsc[:, sc2:sc2 + 1], xnb, AX, apply_absolute_value=True), waits=[t1])
            t3 = step("vector", lambda e, sc2=sc2: [
                e.tensor_scalar_max(qsc[:, sc2 + 1:sc2 + 2], qsc[:, sc2:sc2 + 1], 1e-5)][-1],
                waits=[t2])
            t4 = step("vector", lambda e, sc2=sc2: e.reciprocal(
                qsc[:, sc2 + 2:sc2 + 3], qsc[:, sc2 + 1:sc2 + 2]), waits=[t3])
            t5 = step("vector", lambda e, sc2=sc2: e.tensor_scalar_mul(
                qsc[:, sc2 + 3:sc2 + 4], qsc[:, sc2 + 2:sc2 + 3], 127.0), waits=[t4])
            t6 = step("scalar", lambda e, sc2=sc2, xnb=xnb: e.activation(
                dump[:], xnb, AF.Identity, bias=np.float32(MAGIC),
                scale=qsc[:, sc2 + 3:sc2 + 4]),
                waits=[t5] + ([txi_hist[tt - 1]] if tt - 1 in txi_hist else []))
            t6_hist[tt] = t6
            t_xi = step("scalar", lambda e, tt=tt: e.activation(
                xi_bf[:, tt * D:(tt + 1) * D], dump[:], AF.Identity,
                bias=np.float32(-MAGIC)), waits=[t6])
            txi_hist[tt] = t_xi
            t7 = step("vector", lambda e, tt=tt, sc2=sc2: e.tensor_scalar_mul(
                qsc[:, 4 + tt:5 + tt], qsc[:, sc2 + 1:sc2 + 2], 1.0 / 127.0), waits=[t3])
            t8 = step("vector", lambda e, tt=tt: e.tensor_mul(
                qsc[:, 16 + tt:17 + tt], qsc[:, 4 + tt:5 + tt], mcl[:, 0:1]),
                waits=[t7, t_qg])
            t_coef = step("vector", lambda e, tt=tt: e.tensor_mul(
                qsc[:, 20 + tt:21 + tt], qsc[:, 4 + tt:5 + tt], mcl[:, 1:2]),
                waits=[t7, t_qu])
        tev = []
        for tt in range(TT):
            pst = ps_tr0 if tt % 2 == 0 else ps_tr1
            ttr = []
            for c in range(DC):
                w = [t_xi, t_idb]
                if len(tev) >= 9:
                    w.append(tev[-9])
                ttr.append(step("tensor", lambda e, tt=tt, c=c, pst=pst: e.transpose(
                    pst[:, c * 128:(c + 1) * 128],
                    xi_bf[:, tt * D + c * 128:tt * D + (c + 1) * 128], ident_b[:]), waits=w))
            for c in range(DC):
                tev.append(step("vector", lambda e, tt=tt, c=c, pst=pst: e.tensor_copy(
                    xiT[:, c * TC + tt * 128:c * TC + (tt + 1) * 128],
                    pst[:, c * 128:(c + 1) * 128]), waits=[ttr[-1]]))
        t_xiT = tev[-1]
        t_dbg = None
        if DBG_DUMP and k == 0 and slot_s == 4:
            t_dbg = dstep("gpsimd", lambda e: e.dma_start(dbg_xi, xi_bf[:]), "d_ag", waits=[t_xiT])
            dstep("sync", lambda e: e.dma_start(dbg_r2, r_buf[:]), "d_out", waits=[t_sx])
        t_gu = []
        tg_hist = []
        mm_g_by_io = {}
        mm_u_by_io = {}
        gi = 0
        for io in range(8):
            p2 = io % 2
            wbg = wgubuf[2 * p2]
            w = [t_qg, t_qu, t_sx]
            if io - 2 in mm_g_by_io:
                w.append(mm_g_by_io[io - 2])
            t_wga = dstep("sync", lambda e, io=io, wbg=wbg: e.dma_start(
                wbg[:, 0:2048].rearrange("p (c n) -> p c n", c=4),
                wgq_d.ap().rearrange("(c p) n -> p c n", p=128)
                [:, 0:4, io * 512:(io + 1) * 512]), "d_wgu0", waits=w)
            t_wgb = dstep("sync", lambda e, io=io, wbg=wbg: e.dma_start(
                wbg[:, 2048:4096].rearrange("p (c n) -> p c n", c=4),
                wgq_d.ap().rearrange("(c p) n -> p c n", p=128)
                [:, 4:8, io * 512:(io + 1) * 512]), "d_wd0", waits=list(w))
            wbu = wgubuf[2 * p2 + 1]
            wu = [t_qu, t_sx] + ([mm_u_by_io[io - 2]] if io - 2 in mm_u_by_io else [])
            t_wua = dstep("sync", lambda e, io=io, wbu=wbu: e.dma_start(
                wbu[:, 0:2048].rearrange("p (c n) -> p c n", c=4),
                wuq_d.ap().rearrange("(c p) n -> p c n", p=128)
                [:, 0:4, io * 512:(io + 1) * 512]), "d_wgu1", waits=wu)
            t_wub = dstep("sync", lambda e, io=io, wbu=wbu: e.dma_start(
                wbu[:, 2048:4096].rearrange("p (c n) -> p c n", c=4),
                wuq_d.ap().rearrange("(c p) n -> p c n", p=128)
                [:, 4:8, io * 512:(io + 1) * 512]), "d_wd1", waits=list(wu))
            for tt in range(TT):
                p = gi % 2
                gb = gatebuf[:, 0:512] if p == 0 else xn3[:, 0:512]
                mmg = None
                for c in range(DC):
                    w0 = [t_wga if c < 4 else t_wgb, t_xiT]
                    if c == 0 and len(tg_hist) >= 2:
                        w0.append(tg_hist[-2])
                    mmg = step("tensor", lambda e, tt=tt, c=c, p=p, wbg=wbg: e.matmul(
                        ps_m[p][:], xiT[:, c * TC + tt * 128:c * TC + (tt + 1) * 128],
                        wbg[:, c * 512:(c + 1) * 512], start=(c == 0), stop=(c == DC - 1)),
                        waits=w0)
                tg = step("scalar", lambda e, tt=tt, p=p, gb=gb: e.activation(
                    gb, ps_m[p][:], AF.Silu, scale=qsc[:, 16 + tt:17 + tt]),
                    waits=[mmg, t_coef] + ([t_gu[-2]] if len(t_gu) >= 2 else []))
                tg_hist.append(tg)
                mmu = None
                for c in range(DC):
                    mmu = step("tensor", lambda e, tt=tt, c=c, p=p, wbu=wbu: e.matmul(
                        ps_x[p][:], xiT[:, c * TC + tt * 128:c * TC + (tt + 1) * 128],
                        wbu[:, c * 512:(c + 1) * 512], start=(c == 0), stop=(c == DC - 1)),
                        waits=[t_wua if c < 4 else t_wub]
                        + ([t_gu[-2]] if c == 0 and len(t_gu) >= 2 else []))
                t_gu.append(step("vector", lambda e, tt=tt, io=io, p=p, gb=gb:
                                 e.scalar_tensor_tensor(
                    gu[:, tt * INNER + io * 512:tt * INNER + (io + 1) * 512], ps_x[p][:],
                    qsc[:, 20 + tt:21 + tt], gb, OP.mult, OP.mult),
                    waits=[mmu, tg]))
                if tt == TT - 1:
                    mm_g_by_io[io] = mmg
                    mm_u_by_io[io] = mmu
                gi += 1
        t_dbg2 = None
        if DBG_DUMP and k == 0 and slot_s == 4:
            t_dbg2 = dstep("sync", lambda e: e.dma_start(dbg_gu, gu[:]), "d_out", waits=[t_gu[-1]])
        ev2 = []
        wd_last = {0: None, 1: None}
        tq_hist = {}
        ttr_hist = {}
        tx2_hist = {}
        t_cds = {}
        t_x2s = {}
        dn_bank = {(0, 0): ps_m[0], (1, 0): ps_m[1], (0, 1): ps_x[0], (1, 1): ps_x[1]}
        ev_by_bank = {}
        wds = 0

        def quant2_part(tt):
            sc = 0 if tt % 2 == 0 else 12
            q2 = q2stg if tt % 2 == 0 else q2stg2
            w1 = [t_gu[-1]]
            if tt - 2 in tq_hist:
                w1.append(tq_hist[tt - 2][-1])
            t1 = step("vector", lambda e, tt=tt, sc=sc: e.reduce_max(
                qsc[:, sc:sc + 1], gu[:, tt * INNER:(tt + 1) * INNER], AX,
                apply_absolute_value=True), waits=w1)
            t2 = step("vector", lambda e, sc=sc: e.tensor_scalar_max(
                qsc[:, sc + 1:sc + 2], qsc[:, sc:sc + 1], 1e-5), waits=[t1])
            t3 = step("vector", lambda e, sc=sc: e.reciprocal(
                qsc[:, sc + 2:sc + 3], qsc[:, sc + 1:sc + 2]), waits=[t2])
            t4 = step("vector", lambda e, sc=sc: e.tensor_scalar_mul(
                qsc[:, sc + 3:sc + 4], qsc[:, sc + 2:sc + 3], 127.0), waits=[t3])
            t5 = step("vector", lambda e, tt=tt, sc=sc: e.tensor_scalar_mul(
                qsc[:, 4 + tt:5 + tt], qsc[:, sc + 1:sc + 2], 1.0 / 127.0), waits=[t2])
            t_cds[tt] = step("vector", lambda e, tt=tt: e.tensor_mul(
                qsc[:, 24 + tt:25 + tt], qsc[:, 4 + tt:5 + tt], mcl[:, 2:3]),
                waits=[t5, t_qd])
            tq = []
            for ih in range(4):
                wa = [t4] + ([tq[-1]] if tq else [])
                if ih == 0 and tt - 1 in tq_hist:
                    wa.append(tq_hist[tt - 1][-1])
                if ih == 0 and tt - 2 in ttr_hist:
                    wa.append(ttr_hist[tt - 2])
                if ih == 0 and tt == 1:
                    wa.append(mm_u_by_io[6])
                if t_dbg is not None and tt == 0 and ih == 0:
                    wa += [t_dbg, t_dbg2]
                ta = step("scalar", lambda e, tt=tt, ih=ih, sc=sc: e.activation(
                    dump[:],
                    gu[:, tt * INNER + ih * 1024:tt * INNER + (ih + 1) * 1024],
                    AF.Identity, bias=np.float32(MAGIC), scale=qsc[:, sc + 3:sc + 4]),
                    waits=wa)
                tq.append(step("scalar", lambda e, tt=tt, ih=ih, q2=q2: e.activation(
                    q2[:, ih * 1024:(ih + 1) * 1024], dump[:],
                    AF.Identity, bias=np.float32(-MAGIC)), waits=[ta]))
            tq_hist[tt] = tq
            return tq

        def transpose_part(tt):
            tq = tq_hist[tt]
            q2 = q2stg if tt % 2 == 0 else q2stg2
            xt2 = xiT2[tt % 2]
            ttr = None
            tx2 = []
            for ci in range(IC):
                pst = ps_tr0 if (ci // 4) % 2 == 0 else ps_tr1
                w = [tq[ci // 8], t_idb]
                if ci == 0 and tt - 1 in tx2_hist:
                    w.append(tx2_hist[tt - 1])
                if ci == 0 and tt == 1:
                    w.append(mm_g_by_io[6])
                if ci == 0 and tt >= 2:
                    w.append(wd_last[1])
                if len(tx2) >= 2 and ci % 4 == 0:
                    w.append(tx2[-2])
                ttr = step("tensor", lambda e, ci=ci, pst=pst, q2=q2: e.transpose(
                    pst[:, (ci % 4) * 128:(ci % 4 + 1) * 128],
                    q2[:, ci * 128:(ci + 1) * 128], ident_b[:]), waits=w)
                if ci % 4 == 3:
                    tx2.append(step("vector",
                                    lambda e, ci=ci, xt2=xt2, pst=pst: e.tensor_copy(
                        xt2[:, (ci - 3) * 128:(ci + 1) * 128], pst[:, 0:512]),
                        waits=[ttr]))
            ttr_hist[tt] = ttr
            tx2_hist[tt] = tx2[-1]
            t_x2s[tt] = tx2[-1]

        quant2_part(0)
        quant2_part(1)
        transpose_part(0)
        transpose_part(1)
        quant2_part(2)
        quant2_part(3)
        for pair in range(2):
            if pair == 1:
                transpose_part(2)
                transpose_part(3)
            for dh in range(2):
                mm_by_tt = {}
                for g4 in range(4):
                    par = wds % 2
                    wds += 1
                    wbd = wd_buf[:, par * 4096:(par + 1) * 4096]
                    w = [t_qd, t_sx]
                    if wd_last[par] is not None:
                        w.append(wd_last[par])
                    elif par == 0:
                        w.append(mm_g_by_io[7])
                    else:
                        w.append(mm_u_by_io[7])
                    t_wda = dstep("sync", lambda e, dh=dh, g4=g4, wbd=wbd: e.dma_start(
                        wbd[:, 0:2048].rearrange("p (ci n) -> p ci n", ci=4),
                        wdq_d.ap().rearrange("(ci p) n -> p ci n", p=128)
                        [:, g4 * 8:g4 * 8 + 4, dh * 512:(dh + 1) * 512]),
                        "d_wd0" if par == 0 else "d_wd1", waits=w)
                    t_wdb = dstep("sync", lambda e, dh=dh, g4=g4, wbd=wbd: e.dma_start(
                        wbd[:, 2048:4096].rearrange("p (ci n) -> p ci n", ci=4),
                        wdq_d.ap().rearrange("(ci p) n -> p ci n", p=128)
                        [:, g4 * 8 + 4:g4 * 8 + 8, dh * 512:(dh + 1) * 512]),
                        "d_wgu0" if par == 0 else "d_wgu1", waits=list(w))
                    mm = None
                    for tt in (2 * pair, 2 * pair + 1):
                        xt2 = xiT2[tt % 2]
                        bank = dn_bank[(tt % 2, dh)]
                        for ci in range(8):
                            cig = g4 * 8 + ci
                            w0 = [t_wda if ci < 4 else t_wdb, t_x2s[tt]]
                            if cig == 0 and ci == 0:
                                w0.append(ev_by_bank.get((tt % 2, dh), t_gu[-1]))
                            mm = step("tensor", lambda e, ci=ci, cig=cig, xt2=xt2,
                                      bank=bank, wbd=wbd: e.matmul(
                                bank[:], xt2[:, cig * 128:(cig + 1) * 128],
                                wbd[:, ci * 512:(ci + 1) * 512],
                                start=(cig == 0), stop=(cig == IC - 1)),
                                waits=w0)
                        mm_by_tt[tt] = mm
                    wd_last[par] = mm
                for tt in (2 * pair, 2 * pair + 1):
                    bank = dn_bank[(tt % 2, dh)]
                    ev_by_bank[(tt % 2, dh)] = step(
                        "vector", lambda e, tt=tt, dh=dh, bank=bank: e.scalar_tensor_tensor(
                            r_buf[:, tt * D + dh * 512:tt * D + (dh + 1) * 512], bank[:],
                            qsc[:, 24 + tt:25 + tt],
                            r_buf[:, tt * D + dh * 512:tt * D + (dh + 1) * 512],
                            OP.mult, OP.add,
                            accum_out=colm[:, 40 + tt * 2 + dh:41 + tt * 2 + dh]),
                        waits=[mm_by_tt[tt], t_cds[tt]])
                    ev2.append(ev_by_bank[(tt % 2, dh)])
        if not stats:
            return ev2[-1], ev2[-1], tq_hist[3][-1]
        t_sx2 = step("vector", lambda e: e.tensor_add(
            sumx[:], colm[:].rearrange("p (a b) -> p a b", a=32, b=2)[:, 20:24, 0],
            colm[:].rearrange("p (a b) -> p a b", a=32, b=2)[:, 20:24, 1]), waits=[ev2[-1]])
        tsq = None
        for tt in range(TT):
            tsq = step("scalar", lambda e, tt=tt: e.activation(
                dump[:], r_buf[:, tt * D:(tt + 1) * D], AF.Square,
                accum_out=sumsq[:, tt:tt + 1]), waits=[ev2[-1]] + ([tsq] if tsq else []))
        return t_sx2, tsq

    def stage_D(k, t_sx, t_sq):
        toks = ln_stats(t_sx, t_sq, None, None, extra="attn_b")
        t_ln = toks[-1]
        t_ln2 = toks[-2]
        t_u = []
        for tt in range(TT):
            t_u.append(step("scalar", lambda e, tt=tt: e.activation(
                tokmaj[:, tt * D:(tt + 1) * D], r_buf[:, tt * D:(tt + 1) * D], AF.Identity,
                bias=sln[:, 7 * TT + tt:7 * TT + tt + 1],
                scale=sln[:, 6 * TT + tt:6 * TT + tt + 1]), waits=[t_ln, t_ln2]))
        tev = []
        for tt in range(TT):
            for c in range(DC):
                pst = ps_x[c % 2]
                w = [t_u[tt], t_id]
                if len(tev) >= 2:
                    w.append(tev[-2])
                ttr = step("tensor", lambda e, tt=tt, c=c, pst=pst: e.transpose(
                    pst[:, 0:128], tokmaj[:, tt * D + c * 128:tt * D + (c + 1) * 128],
                    ident_f[:]), waits=w)
                tev.append(step("vector", lambda e, tt=tt, c=c, pst=pst: e.tensor_scalar(
                    xnT_b[:, c * TC + tt * 128:c * TC + (tt + 1) * 128], pst[:, 0:128],
                    vp_col("gamma_b", c), vp_col("beta_b", c), OP.mult, OP.add),
                    waits=[ttr, t_vp]))
        t_xnb = tev[-1]
        t_xnbb = step("vector", lambda e: e.tensor_copy(xnTb_bf[:], xnT_b[:]), waits=[t_xnb])
        mm = None
        for c in range(DC):
            mm = step("tensor", lambda e, c=c: e.matmul(
                ps_sm[0:64, :], wbc_b[:, c * 64:(c + 1) * 64], xnTb_bf[:, c * TC:(c + 1) * TC],
                start=(c == 0), stop=(c == DC - 1)), waits=[t_xnbb, t_wbc])
        t_bc = step("vector", lambda e: e.tensor_scalar(
            bcstg[0:64, :], ps_sm[0:64, :], vp_col("bcb")[0:64, :], None, OP.add), waits=[mm, t_vp])
        t_d1 = dstep("sync", lambda e: e.dma_start(bc_d.ap(), bcstg[0:64, :]), "d_bcs", waits=[t_bc])
        t_d2 = dstep("sync", lambda e: e.dma_start(
            bcsrcB[:].rearrange("p (a n) -> p a n", a=6)[0:65:32, :, :],
            bc_d.ap()[0:18, :].rearrange("(a b) n -> b a n", b=3)), "d_bcl", waits=[t_d1])
        t_d3 = dstep("sync", lambda e: e.dma_start(
            bcsrcC[:].rearrange("p (a n) -> p a n", a=6)[0:65:32, :, :],
            bc_d.ap()[32:50, :].rearrange("(a b) n -> b a n", b=3)), "d_bcl", waits=[t_d1])
        t_d2 = t_d3
        tbc = []
        for n in range(N16):
            a, b2 = n // 3, n % 3
            srcB = bcsrcB[32 * b2:32 * b2 + 1, a * 512:(a + 1) * 512]
            srcC = bcsrcC[32 * b2:32 * b2 + 1, a * 512:(a + 1) * 512]
            w = [t_d2, t_onb]
            if len(tbc) >= 2:
                w.append(tbc[-2])
            t1 = step("tensor", lambda e, srcB=srcB, b2=b2: e.matmul(
                ps_m[0][:], ones3_b[32 * b2:32 * b2 + 1, :], srcB, start=True, stop=True), waits=w)
            tbc.append(step("vector", lambda e, n=n: e.tensor_copy(
                bcastB[:, n * TC:(n + 1) * TC], ps_m[0][:]), waits=[t1]))
            w = [t_d3, t_onb]
            if len(tbc) >= 2:
                w.append(tbc[-2])
            t2 = step("tensor", lambda e, srcC=srcC, b2=b2: e.matmul(
                ps_m[1][:], ones3_b[32 * b2:32 * b2 + 1, :], srcC, start=True, stop=True), waits=w)
            tbc.append(step("scalar", lambda e, n=n: e.activation(
                bcastC[:, n * TC:(n + 1) * TC], ps_m[1][:], AF.Identity), waits=[t2]))
        t_bcast = tbc[-1]
        t_bcastB = tbc[-2]
        t_r3p = []
        for tt in range(TT):
            t_r3p.append(step("scalar", lambda e, tt=tt: e.activation(
                tokmaj[:, tt * D:(tt + 1) * D], r_buf[:, tt * D:(tt + 1) * D], AF.Identity,
                bias=sln[:, 9 * TT + tt:9 * TT + tt + 1],
                scale=sln[:, 8 * TT + tt:8 * TT + tt + 1]), waits=[t_xnb, t_ln, t_ln2]))
        prev_y = None
        prev_scan = None
        prev_add = None
        prev_tch = None
        padds = {}
        t1_prev = None
        mm_prev = None
        for c in range(DC):
            w = [t_wdel, t_sx, t_bcast, t_bcastB]
            if mm_prev is not None:
                w.append(mm_prev)
            t_wd2 = dstep("sync", lambda e, c=c: e.dma_start(
                wdelbuf[:].rearrange("p (ci n) -> p ci n", ci=DC),
                w_delta_d.ap().rearrange("(ci p) n -> p ci n", p=128)[:, :, c * 128:(c + 1) * 128]),
                "d_wdel", waits=w)
            mm = None
            for ci in range(DC):
                w0 = [t_wd2, t_xnbb]
                if t1_prev is not None and ci == 0:
                    w0.append(t1_prev)
                mm = step("tensor", lambda e, ci=ci: e.matmul(
                    ps_m[0][:], wdelbuf[:, ci * 128:(ci + 1) * 128],
                    xnTb_bf[:, ci * TC:(ci + 1) * TC], start=(ci == 0), stop=(ci == DC - 1)),
                    waits=w0)
            mm_prev = mm
            t1 = step("scalar", lambda e, c=c: e.activation(dump[:, 0:TC], ps_m[0][:], AF.Exp,
                                                            bias=vp_col("b_delta", c)), waits=[mm])
            t1_prev = t1
            t2 = step("scalar", lambda e: e.activation(dsm[:, 0:TC], dump[:, 0:TC], AF.Ln,
                                                       bias=np.float32(1.0)), waits=[t1])
            t3 = step("vector", lambda e, c=c: e.tensor_mul(
                dsm[:, TC:2 * TC], dsm[:, 0:TC], xnT_b[:, c * TC:(c + 1) * TC]),
                waits=[t2, t_xnb])
            for half in range(NSUB):
                h0 = half * TSUB
                w = [t2]
                if prev_tch is not None:
                    w.append(prev_tch)
                tz = step("scalar", lambda e: e.activation(
                    slabA[:].rearrange("p (n w) -> p n w", n=N16)[:, :, 0],
                    slabA[:].rearrange("p (n w) -> p n w", n=N16)[:, :, 0],
                    AF.Identity, scale=np.float32(0.0)), waits=w)
                tdA = None
                for n in range(N16):
                    tdA = step("scalar", lambda e, c=c, n=n, h0=h0: e.activation(
                        slabA[:, n * (TSUB + 1) + 1:(n + 1) * (TSUB + 1)],
                        dsm[:, h0:h0 + TSUB], AF.Exp,
                        scale=A_b[:, c * N16 + n:c * N16 + n + 1]), waits=[tz, t_ab])
                if k == 0 and c == 0 and half == 0:
                    rwait("vector", "vWB", "sAGb")
                w = [t3]
                if prev_scan is not None:
                    w.append(prev_scan)
                if prev_y is not None:
                    w.append(prev_y)
                tg2 = step("vector", lambda e, c=c: e.tensor_copy(
                    slabB[:].rearrange("p (n w) -> p n w", n=N16)[:, :, 0],
                    carryB[:, c * N16:(c + 1) * N16]),
                    waits=w + ([t_init] if k == 0 and c == 0 and half == 0 else []))
                tb = None
                for n in range(N16):
                    tb = step("vector", lambda e, c=c, n=n, h0=h0: e.tensor_mul(
                        slabB[:, n * (TSUB + 1) + 1:(n + 1) * (TSUB + 1)],
                        dsm[:, TC + h0:TC + h0 + TSUB],
                        bcastB[:, n * TC + h0:n * TC + h0 + TSUB]), waits=[tg2, t_bcast])
                tscan = step("vector", lambda e: e.tensor_tensor_scan(
                    slabA[:], slabA[:], slabB[:], 0.0, OP.mult, OP.add), waits=[tdA, tb])
                prev_scan = tscan
                tc2 = step("vector", lambda e, c=c: e.tensor_copy(
                    carryB[:, c * N16:(c + 1) * N16],
                    slabA[:].rearrange("p (n w) -> p n w", n=N16)[:, :, TSUB]), waits=[tscan])
                if k == min(NCH, DBG_CHUNKS) - 1 and c == DC - 1 and half == NSUB - 1:
                    emit("vector", lambda e: e.engine_nop(), waits=[tc2], inc="sCb", amt=1)
                tch = None
                for n in range(N16):
                    tch = step("vector", lambda e, c=c, n=n, h0=h0: e.tensor_mul(
                        slabB[:, n * TSUB:(n + 1) * TSUB],
                        slabA[:, n * (TSUB + 1) + 1:(n + 1) * (TSUB + 1)],
                        bcastC[:, n * TC + h0:n * TC + h0 + TSUB]),
                        waits=[tc2, t_bcast] if n == 0 else [])
                prev_tch = tch
                tw = tch
                for wdt in (8, 4, 2):
                    tw = step("vector", lambda e, wdt=wdt: e.tensor_add(
                        slabB[:, 0:wdt * TSUB], slabB[:, 0:wdt * TSUB],
                        slabB[:, wdt * TSUB:2 * wdt * TSUB]), waits=[tw])
                prev_y = step("vector", lambda e, h0=h0: e.tensor_add(
                    dsm[:, 2 * TC + h0:2 * TC + h0 + TSUB],
                    slabB[:, 0:TSUB], slabB[:, TSUB:2 * TSUB]), waits=[tw])
            t_yd = step("vector", lambda e, c=c: e.scalar_tensor_tensor(
                dsm[:, 2 * TC:3 * TC], xnT_b[:, c * TC:(c + 1) * TC], vp_col("Dsk_b", c),
                dsm[:, 2 * TC:3 * TC], OP.mult, OP.add), waits=[prev_y])
            for tt in range(TT):
                bank = ps_h if tt % 2 == 0 else ps_sm
                w = [t_yd, t_id, t_r3p[tt]]
                if padds.get(tt % 2) is not None:
                    w.append(padds[tt % 2])
                ttr = step("tensor", lambda e, c=c, tt=tt, bank=bank: e.transpose(
                    bank[:, 0:128], dsm[:, 2 * TC + tt * 128:2 * TC + (tt + 1) * 128],
                    ident_f[:]), waits=w)
                prev_add = step("vector", lambda e, c=c, tt=tt, bank=bank: e.tensor_add(
                    tokmaj[:, tt * D + c * 128:tt * D + (c + 1) * 128],
                    tokmaj[:, tt * D + c * 128:tt * D + (c + 1) * 128],
                    bank[:, 0:128]), waits=[ttr])
                padds[tt % 2] = prev_add
            prev_y = prev_add
        t_mv = None
        for tt in range(TT):
            for dh in range(2):
                t_mv = step("vector", lambda e, tt=tt, dh=dh: e.scalar_tensor_tensor(
                    r_buf[:, tt * D + dh * 512:tt * D + (dh + 1) * 512],
                    tokmaj[:, tt * D + dh * 512:tt * D + (dh + 1) * 512], 1.0,
                    tokmaj[:, tt * D + dh * 512:tt * D + (dh + 1) * 512], OP.mult, OP.bypass,
                    accum_out=colm[:, 40 + tt * 2 + dh:41 + tt * 2 + dh]), waits=[prev_add])
        t_sx2 = step("vector", lambda e: e.tensor_add(
            sumx[:], colm[:].rearrange("p (a b) -> p a b", a=32, b=2)[:, 20:24, 0],
            colm[:].rearrange("p (a b) -> p a b", a=32, b=2)[:, 20:24, 1]), waits=[t_mv])
        tsq = None
        for tt in range(TT):
            tsq = step("scalar", lambda e, tt=tt: e.activation(
                dump[:], r_buf[:, tt * D:(tt + 1) * D], AF.Square,
                accum_out=sumsq[:, tt:tt + 1]), waits=[t_mv] + ([tsq] if tsq else []))
        return t_sx2, tsq

    if DBG_CUT >= 4 and not (30 <= DBG_CUT < 40):
        ag_block("sCa", "gKA", ag_a_in, ag_a_out, agoA, carryA, DC, "sAGa", t_init)
    if stages >= 3 and DBG_CUT >= 6:
        ag_block("sCb", "gKB", ag_b_in, ag_b_out, agoB, carryB, 128, "sAGb", t_init)

    if DBG_CUT < 2:
        t_final = dstep("sync", lambda e: e.dma_start(
            out_sh[0:TC, :].rearrange("(tt p) d -> p tt d", p=128),
            r_buf[:].rearrange("p (tt d) -> p tt d", tt=TT)), "d_out", waits=[t_prologue])
        emit("sync", lambda e: e.nop(), waits=[t_final], inc="sync_c", amt=1)
        return _finish(nc, q)
    th0 = halo_token(0, waits=[t_prologue])
    hv = None
    hv_prev = None
    for dp in range(4):
        wb = wgubuf[dp % 2]
        w = [t_win, th0]
        if hv_prev is not None:
            w.append(hv_prev)
        t_ws = dstep("sync", lambda e, dp=dp, wb=wb: e.dma_start(
            wb[:, 0:2048].rearrange("p (c n) -> p c n", c=DC),
            w_in_d.ap().rearrange("(c p) n -> p c n", p=128)[:, :, dp * 256:(dp + 1) * 256]),
            "d_wgu0" if dp % 2 == 0 else "d_wgu1", waits=w)
        for j in range(2):
            for c in range(DC):
                hv = step("tensor", lambda e, dp=dp, c=c, j=j, wb=wb: e.matmul(
                    ps_h[:, 2 * dp + j:2 * dp + j + 1],
                    wb[:, c * 256 + j * 128:c * 256 + (j + 1) * 128],
                    xnTh[:, c:c + 1], start=(c == 0), stop=(c == DC - 1)),
                    waits=[t_ws] if c == 0 else [])
        hv_prev = hv
    t_h0 = step("vector", lambda e: e.tensor_scalar(
        xbh_prev[:], ps_h[:, 0:DC], vp_col("hm", 0), None, OP.mult), waits=[hv, t_vp, t_init])

    t_entry = [t_h0]
    t_early = None
    t_outprev = None
    for k in range(min(NCH, DBG_CHUNKS)):
        if DBG_CUT < 3:
            break
        tA, tHev, tLh, tSv, a_evs = stage_A(k, t_entry, t_early)
        tsx, tsq = (tA, tHev)
        t_r1l = tA
        if DBG_CUT >= 4 and not (30 <= DBG_CUT < 40):
            tsx, tsq, t_r1l = stage_B(k, tA, tHev, tLh, t_outprev, a_evs)
        if stages >= 2 and DBG_CUT >= 5 and not (30 <= DBG_CUT < 40):
            tsx, tsq = stage_FFN(k, tsx, tsq, 4, 5)
        if stages >= 3 and DBG_CUT >= 6 and not (30 <= DBG_CUT < 40):
            tsx, tsq = stage_D(k, tsx, tsq)
        if stages >= 4 and DBG_CUT >= 7 and not (30 <= DBG_CUT < 40):
            tsx, tsq, t_aln = stage_FFN(k, tsx, tsq, 10, 11, stats=False)
        t_out = dstep("sync", lambda e, k=k: e.dma_start(
            out_sh[k * TC:(k + 1) * TC, :].rearrange("(tt p) d -> p tt d", p=128),
            r_buf[:].rearrange("p (tt d) -> p tt d", tt=TT)), "d_out", waits=[tsx, tsq, tSv])
        t_outprev = t_out
        t_entry = [t_r1l, tsx, tsq]
        t_early = [t_r1l, t_aln]
    t_done = t_outprev if t_outprev is not None else t_h0

    emit("sync", lambda e: e.nop(), waits=[t_done], inc="sync_c", amt=1)
    return _finish(nc, q)


def _finish(nc, q):
    with nc.Block() as block:
        @block.sync
        def _(eng):
            for f in q["sync"]:
                f(eng)

        @block.scalar
        def _(eng):
            for f in q["scalar"]:
                f(eng)

        @block.vector
        def _(eng):
            for f in q["vector"]:
                f(eng)

        @block.tensor
        def _(eng):
            for f in q["tensor"]:
                f(eng)

        @block.gpsimd
        def _(eng):
            for f in q["gpsimd"]:
                f(eng)

    return nc


def _prep_inputs(inputs):
    f = np.float32
    B, L = 4, 8192
    x = np.ascontiguousarray(np.asarray(inputs["L0_attn_a_x"], f))

    vpar = np.zeros((128, NV), f)

    def put(name, arr):
        c0, w = VP.cols[name]
        vpar[:, c0:c0 + w] = arr

    def fm(v):
        return np.ascontiguousarray(np.asarray(v, f).reshape(DC, 128).T)

    put("gamma_a", fm(inputs["L0_attn_a_gamma"]))
    put("beta_a", fm(inputs["L0_attn_a_beta"]))
    cw = np.asarray(inputs["L0_attn_a_conv_w"], f).reshape(3, D)
    cwf = np.zeros((128, 3 * DC), f)
    for c in range(DC):
        for kk in range(3):
            cwf[:, c * 3 + kk] = cw[kk, c * 128:(c + 1) * 128]
    put("conv_w", cwf)
    put("conv_b", fm(inputs["L0_attn_a_conv_b"]))
    put("b_dt", fm(inputs["L0_attn_a_b_dt"]))
    put("A_a", fm(inputs["L0_attn_a_A"]))
    put("Dsk_a", fm(inputs["L0_attn_a_D_skip"]))
    put("gamma_b", fm(inputs["L0_attn_b_gamma"]))
    put("beta_b", fm(inputs["L0_attn_b_beta"]))
    put("b_delta", fm(inputs["L0_attn_b_b_delta"]))
    put("Dsk_b", fm(inputs["L0_attn_b_D_skip"]))
    al = np.asarray(inputs["L0_attn_b_A_log"], f)
    alf = np.zeros((128, DC * N16), f)
    for c in range(DC):
        alf[:, c * N16:(c + 1) * N16] = al[c * 128:(c + 1) * 128, :]
    put("A_logb", alf)
    bcb = np.zeros((128, 1), f)
    bcb[0:16, 0] = np.asarray(inputs["L0_attn_b_b_B"], f)
    bcb[32:48, 0] = np.asarray(inputs["L0_attn_b_b_C"], f)
    put("bcb", bcb)
    put("onescol", np.ones((128, 1), f))

    lr = np.zeros((D, 128), f)
    lr[:, 0:64] = np.asarray(inputs["L0_attn_a_W_xp"], f)
    lr[:, 64:80] = np.asarray(inputs["L0_attn_a_W_Bg"], f)
    lr[:, 96:112] = np.asarray(inputs["L0_attn_a_W_Cg"], f)
    wbc = np.zeros((D, 64), f)
    wbc[:, 0:16] = np.asarray(inputs["L0_attn_b_W_B"], f)
    wbc[:, 32:48] = np.asarray(inputs["L0_attn_b_W_C"], f)

    shared = {
        "w_in": np.ascontiguousarray(np.asarray(inputs["L0_attn_a_W_in"], f)),
        "lr_w": np.ascontiguousarray(lr),
        "w_dt": np.ascontiguousarray(np.asarray(inputs["L0_attn_a_W_dt"], f)),
        "w_bp": np.ascontiguousarray(np.asarray(inputs["L0_attn_a_W_Bp"], f)),
        "w_cp": np.ascontiguousarray(np.asarray(inputs["L0_attn_a_W_Cp"], f)),
        "w_out": np.ascontiguousarray(np.asarray(inputs["L0_attn_a_W_out"], f)),
        "w_delta": np.ascontiguousarray(np.asarray(inputs["L0_attn_b_W_delta"], f)),
        "w_bc": np.ascontiguousarray(wbc),
        "wg_T": np.ascontiguousarray(np.asarray(inputs["L0_ffn_w_gate"], f).T),
        "wu_T": np.ascontiguousarray(np.asarray(inputs["L0_ffn_w_up"], f).T),
        "wd_T": np.ascontiguousarray(np.asarray(inputs["L0_ffn_w_down"], f).T),
        "ident_in": np.eye(128, dtype=f),
        "ones_row": np.ones((1, 128), f),
    }

    in_maps = []
    for core in range(8):
        b, h = core // 2, core % 2
        lo = h * TLOC
        xs = np.zeros((TLOC + 2, D), f)
        xs[1:TLOC + 1] = x[b, lo:lo + TLOC]
        if lo > 0:
            xs[0] = x[b, lo - 1]
        if lo + TLOC < L:
            xs[TLOC + 1] = x[b, lo + TLOC]
        vpc = vpar.copy()
        c0, _ = VP.cols["hm"]
        vpc[:, c0] = 0.0 if h == 0 else 1.0
        vpc[:, c0 + 1] = 0.0 if h == 1 else 1.0
        c0, _ = VP.cols["msk"]
        vpc[:, c0] = 1.0 if h == 1 else 0.0
        odd = (h == 1)
        cfgv = np.zeros((1, 8), np.int32)
        cfgv[0, 0] = 1 if odd else 0
        cfgv[0, 1] = 1 if odd else 0
        cfgv[0, 2] = 0 if odd else 1
        cfgv[0, 3] = 0 if odd else 1
        m = {"x_sh": xs, "vparams": np.ascontiguousarray(vpc), "cfg": cfgv}
        m.update(shared)
        in_maps.append(m)
    return in_maps


def kernel(**inputs):
    in_maps = _prep_inputs(inputs)
    nc = build_nc(STAGES)
    res = run_bass_kernel_spmd(nc, in_maps, core_ids=list(range(8)))
    out = np.empty((4, 8192, D), np.float32)
    for core in range(8):
        b, h = core // 2, core % 2
        out[b, h * TLOC:(h + 1) * TLOC] = res.results[core]["out_sh"]
    return out
